# version 15
# baseline (speedup 1.0000x reference)
import contextlib
import numpy as np

import concourse.bass as bass
import concourse.mybir as mybir
from concourse.bass_utils import run_bass_kernel_spmd

F32 = mybir.dt.float32
I32 = mybir.dt.int32
ALU = mybir.AluOpType
ACT = mybir.ActivationFunctionType
AXL = mybir.AxisListType

D = 1024
KH = 32
HD = 32
MTH = 4
TOT = 2 * D + KH
FF = 4096
LSEQ = 8192
NCORE = 8
DG = D // 128
ZG = 17
ZLAST = TOT - 16 * 128
FG = FF // 128
NS = 64
OFF15 = 32768.0

C_EPSLN = 0
C_EPSRMS = 1
C_ONE = 2
C_GB0 = 3
C_GB1 = 4
NCST = 6

T_CORE_DEFAULT = 2048
CH_DEFAULT = 512
DEBUG_CC = False
CC_DELAY_PRE = 25
CC_DELAY_POST = 25


def _zsz(mz):
    return 128 if mz < 16 else ZLAST


def _scol(g, m, side):
    return side * 32 + g * 4 + m


class Tok:
    __slots__ = ("sem", "val")

    def __init__(self, sem, val):
        self.sem = sem
        self.val = val


class Prog:
    ENGINES = ("tensor", "vector", "scalar", "gpsimd", "sync")

    def __init__(self, nc, block, es):
        self.nc = nc
        self.block = block
        self.sems = {}
        self.vals = {}
        for e in self.ENGINES:
            self.sems[e] = es.enter_context(nc.semaphore("ms_" + e))
            self.vals[e] = 0
        self.dsems = {}
        self.dvals = {}
        self.es = es

    def dma_sem(self, name):
        if name not in self.dsems:
            self.dsems[name] = self.es.enter_context(self.nc.semaphore("dm_" + name))
            self.dvals[name] = 0
        return self.dsems[name]

    def step(self, engine, fn, waits=(), attach=False):
        dec = getattr(self.block, engine)
        sem = self.sems[engine]

        @dec
        def _(eng, fn=fn, waits=tuple(w for w in waits if w is not None)):
            for t in waits:
                eng.wait_ge(t.sem, t.val)
            r = fn(eng)
            if attach and r is not None:
                r.then_inc(sem, 1)
            else:
                eng.drain().then_inc(sem, 1)

        self.vals[engine] += 1
        return Tok(sem, self.vals[engine])

    def raw(self, engine, fn, waits=()):
        dec = getattr(self.block, engine)
        vals = self.vals

        def bump():
            vals[engine] += 1
            return vals[engine]

        @dec
        def _(eng, fn=fn, waits=tuple(w for w in waits if w is not None)):
            for t in waits:
                eng.wait_ge(t.sem, t.val)
            fn(eng, bump)

        return Tok(self.sems[engine], self.vals[engine])

    def csem(self, name):
        return self.dma_sem("c_" + name)

    def dma(self, queue_engine, stream, transfers, waits=()):
        sem = self.dma_sem(stream)
        dec = getattr(self.block, queue_engine)

        @dec
        def _(eng, transfers=tuple(transfers), waits=tuple(w for w in waits if w is not None)):
            for t in waits:
                eng.wait_ge(t.sem, t.val)
            for tr in transfers:
                kw = tr[2] if len(tr) > 2 else {}
                eng.dma_start(out=tr[0], in_=tr[1], **kw).then_inc(sem, 16)

        self.dvals[stream] += 16 * len(transfers)
        return Tok(sem, self.dvals[stream])



def build_fe(T_core, ch, debug=(), F=None):
    nch = T_core // ch
    if F is None:
        nc = bass.Bass("TRN2", target_bir_lowering=False)
        i_xT = nc.dram_tensor("xT", [D, T_core + 3], F32, kind="ExternalInput").ap()
    else:
        nc = F["nc"]
        i_xT = F["i_xT"]

    i_win = nc.dram_tensor("win", [ZG * 8 * 128, 128], F32, kind="ExternalInput").ap()
    i_wmod = nc.dram_tensor("wmod", [12 * 8 * 128, 512], F32, kind="ExternalInput").ap()
    i_wcv = nc.dram_tensor("wcv", [128, 4 * ZG], F32, kind="ExternalInput").ap()
    i_bcv = nc.dram_tensor("bcv", [128, ZG], F32, kind="ExternalInput").ap()
    i_theta = nc.dram_tensor("thetas", [128, 32], F32, kind="ExternalInput").ap()
    i_sl = nc.dram_tensor("slvec", [32, 2], F32, kind="ExternalInput").ap()
    i_score = nc.dram_tensor("scorev", [32, 1], F32, kind="ExternalInput").ap()
    i_dlog = nc.dram_tensor("dlog", [1, 3], F32, kind="ExternalInput").ap()
    i_crow = nc.dram_tensor("crow", [128, 8], F32, kind="ExternalInput").ap()
    i_hmask = nc.dram_tensor("hmask", [128, 3], F32, kind="ExternalInput").ap()
    i_r32 = nc.dram_tensor("r32", [32, 8 * 128], F32, kind="ExternalInput").ap()
    i_ones1 = nc.dram_tensor("ones_k1", [1, 128], F32, kind="ExternalInput").ap()
    i_onesc = nc.dram_tensor("ones_col", [128, 1], F32, kind="ExternalInput").ap()
    i_tb = nc.dram_tensor("tbase", [32, 2], F32, kind="ExternalInput").ap()
    i_cst = nc.dram_tensor("cst", [128, NCST], F32, kind="ExternalInput").ap()
    if F is not None:
        i_wsl = nc.dram_tensor("wsl", [128, NCORE], F32, kind="ExternalInput").ap()

    if F is None:
        o_s = nc.dram_tensor("s_out", [128, 66], F32, kind="ExternalOutput").ap()
        o_cs = nc.dram_tensor("sp_cs", [NS * 128, T_core], F32, kind="ExternalOutput").ap()
        o_gate = nc.dram_tensor("sp_gate", [8 * 128, T_core], F32, kind="ExternalOutput").ap()
        o_den = nc.dram_tensor("sp_den", [32, T_core], F32, kind="ExternalOutput").ap()
        o_mod = nc.dram_tensor("mod_out", [1, 6 * D], F32, kind="ExternalOutput").ap()
    else:
        o_cs = F["sp_cs"]
        o_gate = F["sp_gate"]
        o_den = F["sp_den"]
    dbg_shapes = {"h": [DG * 128, T_core], "cz": [ZG * 128, T_core],
                  "pw": [32, T_core], "mg": [NS * 128, T_core]}
    dbg = {nm: nc.dram_tensor("dbg_" + nm, dbg_shapes[nm], F32, kind="ExternalOutput").ap()
           for nm in debug}

    es = contextlib.ExitStack()
    with es:
        def sb(name, shape, dt=F32):
            return es.enter_context(nc.sbuf_tensor(name, shape, dt))

        def pt(name):
            return es.enter_context(nc.psum_tensor(name, [128, 512], F32))

        t_wcv = sb("t_wcv", [128, 4 * ZG])
        _ = None
        t_bcv = sb("t_bcv", [128, ZG])
        t_theta = sb("t_theta", [128, 32])
        t_th4 = sb("t_th4", [128, 32])
        t_th8 = sb("t_th8", [128, 32])
        t_invP = sb("t_invP", [128, 32])
        t_negP = sb("t_negP", [128, 32])
        t_sl = sb("t_sl", [32, 2])
        t_sp = sb("t_sp", [32, 1])
        t_nsp = sb("t_nsp", [32, 1])
        t_score = sb("t_score", [32, 1])
        t_dlog = sb("t_dlog", [1, 3])
        t_wpoly = sb("t_wpoly", [128, 4])
        t_crow = sb("t_crow", [128, 8])
        t_silu = sb("t_silu", [128, 8])
        if F is None:
            t_mod = sb("t_mod", [128, 48])
            t_r32 = sb("t_r32", [32, 8 * 128])
            t_ones1 = sb("t_ones1", [1, 128])
            t_onesc = sb("t_onesc", [128, 1])
            t_cst = sb("t_cst", [128, NCST])
        else:
            t_mod = F["t_mod"]
            t_r32 = F["t_r32"]
            t_ones1 = F["t_ones1"]
            t_onesc = F["t_onesc"]
            t_cst = F["t_cst"]
        t_ca1 = sb("t_ca1", [128, 8])
        t_hmask = sb("t_hmask", [128, 3])
        t_tb = sb("t_tb", [32, 2])
        t_iota = sb("t_iota", [32, ch])
        t_row = sb("t_row", [1, 16])
        t_carry = sb("t_carry", [128, NS])
        t_dcar = sb("t_dcar", [32, 1])
        t_stat = sb("t_stat", [1, 4 * ch])
        t_hst = sb("t_hst", [1, 12])
        t_db = sb("t_db", [32, 1])
        t_db2 = sb("t_db2", [32, 1])
        t_sfin = sb("t_sfin", [128, 66])
        if F is not None:
            t_wsl = sb("t_wsl", [128, NCORE])
            t_sall8 = sb("t_sall8", [128, 66 * NCORE])

        t_x = [sb(f"t_x{g}", [128, ch + 3]) for g in range(DG)]
        t_h = [sb(f"t_h{g}", [128, ch + 3]) for g in range(DG)]
        t_sq = [sb(f"t_sq{g}", [128, ch + 3]) for g in range(DG)]
        t_z = [sb(f"t_z{mz}", [128, ch + 3]) for mz in range(ZG)]
        t_zst = sb("t_zst", [128, 3 * ZG])
        t_cz2 = [sb(f"t_cz{mz}", [128, ch]) for mz in range(8, ZG)]
        t_cz = [t_h[g] for g in range(8)] + t_cz2
        t_gate = t_sq
        t_mg = [sb(f"t_mg{i}", [128, ch]) for i in range(8)]
        t_cs = [sb(f"t_cs{i}", [128, ch]) for i in range(8)]
        if ch >= 512:
            t_mrow = [t_cs[p] for p in range(2)]
        else:
            t_mrow = [sb(f"t_mrow{p}", [1, 512]) for p in range(2)]
        t_dencs = sb("t_dencs", [32, ch])
        t_tw = sb("t_tw", [32, ch])
        t_pw = sb("t_pw", [32, ch])
        t_pwa = sb("t_pwa", [32, ch])
        t_pwb = sb("t_pwb", [32, ch])
        t_pp = [sb(f"t_pp{p}", [128, ch]) for p in range(2)]
        t_ta = [sb(f"t_ta{m}", [128, ch]) for m in range(MTH)]
        t_ki = sb("t_ki", [128, ch], I32)
        t_ki2 = sb("t_ki2", [128, ch], I32)
        t_pih = sb("t_pih", [128, 1])
        t_s8 = [sb(f"t_s8{m}", [128, ch]) for m in range(MTH)]
        t_s4 = [sb(f"t_s4{m}", [128, ch]) for m in range(MTH)]
        t_e = sb("t_e", [128, ch])
        t_f = sb("t_f", [128, ch])
        t_win = [sb(f"t_win{i}", [128, 512]) for i in range(4)]
        if ch >= 512:
            t_wmodb = [t_mg[i][:, 0:512].tensor for i in range(4)]
            t_wmodb = [t_mg[i] for i in range(4)]
        else:
            t_wmodb = [sb(f"t_wmodb{i}", [128, 512]) for i in range(4)]

        if F is not None:
            F["loot"] = (t_mg, t_cs, t_ki, t_ki2)

        pZ = [pt("pZ0"), pt("pZ1")]
        pS1 = pt("pS1")
        pS2 = pt("pS2")
        pR = [pt("pR0"), pt("pR1")]
        pP = [pt("pP0"), pt("pP1")]

        if F is None:
            block = es.enter_context(nc.Block())
            P = Prog(nc, block, es)
        else:
            block = F["block"]
            P = F["P"]

        tk_small = P.dma("sync", "fesmall", [
            (t_wcv[:], i_wcv), (t_bcv[:], i_bcv), (t_theta[:], i_theta),
            (t_sl[:], i_sl), (t_score[:], i_score), (t_dlog[:], i_dlog),
            (t_crow[:], i_crow), (t_hmask[:], i_hmask), (t_r32[:], i_r32),
            (t_ones1[:], i_ones1), (t_onesc[:], i_onesc), (t_tb[:], i_tb),
            (t_cst[:], i_cst),
        ] + ([(t_wsl[:], i_wsl)] if F is not None else []))

        def _su_g(eng):
            eng.iota(t_iota[:], [[1, ch]], base=0, channel_multiplier=0, allow_small_or_imprecise_dtypes=True)
            eng.memzero(t_carry[:])
            eng.memzero(t_dcar[:])
        tk_iota = P.step("gpsimd", _su_g)

        def _su_s1(eng):
            eng.activation(t_silu[:], t_crow[:], ACT.Silu)
            eng.activation(t_nsp[:], t_sl[:, 0:1], ACT.Exp)
            eng.drain()
            eng.activation(t_sp[:], t_nsp[:], ACT.Ln, bias=t_cst[0:32, C_ONE:C_ONE + 1])
            eng.activation(t_row[0:1, 0:1], t_ones1[0:1, 0:1], ACT.Arctan)
        tk_s1 = P.step("scalar", _su_s1, waits=[tk_small])

        def _su_v1(eng):
            eng.tensor_tensor(t_nsp[:], t_sp[:], t_sl[:, 1:2], ALU.mult)
            eng.tensor_scalar(t_row[0:1, 1:2], t_row[0:1, 0:1], 8.0, None, ALU.mult)
            eng.tensor_scalar(t_row[0:1, 13:14], t_row[0:1, 0:1], 2.0, None, ALU.mult)
            eng.tensor_reduce(t_row[0:1, 2:3], t_dlog[:], AXL.X, ALU.max)
            eng.drain()
            eng.tensor_scalar(t_row[0:1, 3:4], t_row[0:1, 2:3], -1.0, None, ALU.mult)
            eng.reciprocal(t_row[0:1, 4:5], t_row[0:1, 1:2])
        tk_v1 = P.step("vector", _su_v1, waits=[tk_s1])

        def _su_s2(eng):
            eng.activation(t_row[0:1, 5:8], t_dlog[:], ACT.Exp, bias=t_row[0:1, 3:4])
        tk_s2 = P.step("scalar", _su_s2, waits=[tk_v1])

        def _su_v2(eng):
            eng.tensor_reduce(t_row[0:1, 8:9], t_row[0:1, 5:8], AXL.X, ALU.add)
            eng.drain()
            eng.reciprocal(t_row[0:1, 9:10], t_row[0:1, 8:9])
            eng.drain()
            eng.tensor_scalar(t_row[0:1, 10:13], t_row[0:1, 5:8], t_row[0:1, 9:10], None, ALU.mult)
        tk_v2 = P.step("vector", _su_v2, waits=[tk_s2])

        def _su_t1(eng):
            eng.matmul(pP[0][0:128, 0:3], t_ones1[:], t_row[0:1, 10:13], start=True, stop=False)
            eng.matmul(pP[0][0:128, 3:4], t_ones1[:], t_row[0:1, 4:5],
                       start=False, stop=False)
            return eng.matmul(pP[0][0:128, 4:5], t_ones1[:], t_row[0:1, 13:14],
                              start=False, stop=True)
        tk_t1 = P.step("tensor", _su_t1, waits=[tk_v2], attach=True)

        def _su_v3(eng):
            eng.tensor_copy(t_wpoly[:], pP[0][0:128, 0:4])
            eng.tensor_copy(t_pih[:], pP[0][0:128, 4:5])
            eng.drain()
            eng.tensor_scalar(t_invP[:], t_theta[:], t_wpoly[:, 3:4], None, ALU.mult)
            eng.tensor_scalar(t_th4[:], t_theta[:], 0.25, None, ALU.mult)
            eng.tensor_scalar(t_th8[:], t_theta[:], 0.125, None, ALU.mult)
            eng.drain()
            eng.reciprocal(t_negP[:], t_invP[:])
            eng.drain()
            eng.tensor_scalar(t_negP[:], t_negP[:], -1.0, None, ALU.mult)
        tk_v3 = P.step("vector", _su_v3, waits=[tk_t1])

        tk_cc_ev = {}
        tk_mod_mm = {}
        tk_mc = {}
        for i in range(96):
            cc, k = divmod(i, 8)
            dma_w = [tk_small] if i < 4 else [tk_mod_mm[i - 4]]
            P.dma("sync", "wm%d" % (i % 4),
                  [(t_wmodb[i % 4][0:128, 0:512], i_wmod[i * 128:(i + 1) * 128, :])],
                  waits=dma_w)

            def _mm(eng, cc=cc, k=k, i=i):
                eng.wait_ge(P.dma_sem("wm%d" % (i % 4)), 16 * (i // 4 + 1))
                dst = pP[1] if cc % 2 else pP[0]
                return eng.matmul(dst[0:1, 0:512], t_silu[:, k:k + 1],
                                  t_wmodb[i % 4][0:128, 0:512],
                                  start=(k == 0), stop=(k == 7))
            w = [tk_s1, tk_v3] if i == 0 else ()
            if k == 0 and cc >= 2:
                w = [tk_cc_ev[cc - 2]]
            t = P.step("tensor", _mm, waits=w, attach=True)
            tk_mod_mm[i] = t
            if k == 7:
                def _ev(eng, cc=cc):
                    src = pP[1] if cc % 2 else pP[0]
                    eng.tensor_copy(t_mrow[cc % 2][0:1, 0:512], src[0:1, 0:512])
                tk_cc_ev[cc] = P.step("vector", _ev,
                                      waits=[t, tk_mc.get(cc - 2)])
                trs = ([] if F is not None else
                       [(o_mod[0:1, cc * 512:(cc + 1) * 512], t_mrow[cc % 2][0:1, 0:512])])
                for s_ in range(4):
                    f0 = 512 * cc + 128 * s_
                    col = (f0 // D) * 8 + (f0 % D) // 128
                    trs.append((t_mod[:, col:col + 1], t_mrow[cc % 2][0:1, 128 * s_:128 * (s_ + 1)],
                                dict(allow_slow_non_contiguous=True)))
                tk_mc[cc] = P.dma("sync", "mc%d" % (cc % 2), trs, waits=[tk_cc_ev[cc]])

        def _su_v4(eng):
            eng.tensor_scalar(t_ca1[:], t_mod[:, 8:16], 1.0, None, ALU.add)
        tk_v4 = P.step("vector", _su_v4, waits=[tk_mc[2], tk_mc[3]])

        tk_z_all = {}
        tk_zev = {}
        tk_spill = {}
        tk_gspill = {}
        tk_win_free = {}
        tk_A = {}
        tk_gs_last = [None]
        prev_h = None
        win_cnt = [0]
        prev_sq = None
        prev_B7 = None
        prev_conv = None
        prev_stash = None
        prev_pw = None
        last_spill = {}

        for j in range(nch):
            pp = j % 2
            xw = [prev_h, tk_v4] + ([tk_v3, tk_iota] if j == 0 else [])
            tk_x = P.dma("gpsimd", "x",
                         [(t_x[g][:], i_xT[g * 128:(g + 1) * 128, j * ch:j * ch + ch + 3])
                          for g in range(DG)], waits=xw)

            for pk in range(34):
                p0 = win_cnt[0]
                win_cnt[0] += 1
                ii = pk * 4
                src_ap = i_win[ii * 128:(ii + 4) * 128, 0:128]
                dst = t_win[p0 % 4]
                inap = src_ap.rearrange("(b p) c -> p b c", b=4)
                P.dma("sync", "wn%d" % (p0 % 4), [(dst[:].rearrange("p (b c) -> p b c", b=4),
                                                   inap)],
                      waits=[tk_small] if p0 < 4 else [Tok(P.csem("win"), 4 * (p0 - 4) + 4)])

            def _sqs(eng):
                for g in range(DG):
                    eng.activation(t_sq[g][:, 0:ch + 3], t_x[g][:, 0:ch + 3], ACT.Square)
            tk_sq = P.step("scalar", _sqs, waits=[tk_x, prev_sq, tk_gspill.get(j - 1)])
            prev_sq = tk_sq

            def _sums(eng, j=j):
                for k in range(DG):
                    eng.matmul(pS1[0:1, 0:ch], t_onesc[:], t_x[k][:, 3:3 + ch],
                               start=(k == 0), stop=(k == DG - 1))
                for k in range(DG):
                    r = eng.matmul(pS2[0:1, 0:ch], t_onesc[:], t_sq[k][:, 3:3 + ch],
                                   start=(k == 0), stop=(k == DG - 1))
                if j == 0:
                    for k in range(DG):
                        eng.matmul(pR[0][0:1, 0:3], t_onesc[:], t_x[k][:, 0:3],
                                   start=(k == 0), stop=(k == DG - 1))
                    for k in range(DG):
                        r = eng.matmul(pR[1][0:1, 0:3], t_onesc[:], t_sq[k][:, 0:3],
                                       start=(k == 0), stop=(k == DG - 1))
                return r
            tk_sums = P.step("tensor", _sums, waits=[tk_x, tk_sq, prev_B7], attach=True)

            def _stats(eng, j=j):
                eng.tensor_scalar(t_stat[0:1, 0:ch], pS1[0:1, 0:ch], 1.0 / D, None, ALU.mult)
                eng.tensor_scalar(t_stat[0:1, ch:2 * ch], pS2[0:1, 0:ch], 1.0 / D, None, ALU.mult)
                if j == 0:
                    eng.tensor_scalar(t_hst[0:1, 0:3], pR[0][0:1, 0:3], 1.0 / D, None, ALU.mult)
                    eng.tensor_scalar(t_hst[0:1, 3:6], pR[1][0:1, 0:3], 1.0 / D, None, ALU.mult)
                eng.drain()
                eng.tensor_tensor(t_stat[0:1, 2 * ch:3 * ch], t_stat[0:1, 0:ch],
                                  t_stat[0:1, 0:ch], ALU.mult)
                if j == 0:
                    eng.tensor_tensor(t_hst[0:1, 6:9], t_hst[0:1, 0:3], t_hst[0:1, 0:3], ALU.mult)
                eng.drain()
                eng.tensor_tensor(t_stat[0:1, 3 * ch:4 * ch], t_stat[0:1, ch:2 * ch],
                                  t_stat[0:1, 2 * ch:3 * ch], ALU.subtract)
                if j == 0:
                    eng.tensor_tensor(t_hst[0:1, 9:12], t_hst[0:1, 3:6], t_hst[0:1, 6:9],
                                      ALU.subtract)
            tk_stats = P.step("vector", _stats, waits=[tk_sums])

            def _sqrtv(eng, j=j):
                eng.activation(t_stat[0:1, 2 * ch:3 * ch], t_stat[0:1, 3 * ch:4 * ch], ACT.Sqrt,
                               bias=t_cst[0:1, C_EPSLN:C_EPSLN + 1])
                if j == 0:
                    eng.activation(t_hst[0:1, 6:9], t_hst[0:1, 9:12], ACT.Sqrt,
                                   bias=t_cst[0:1, C_EPSLN:C_EPSLN + 1])
            tk_sqrt = P.step("scalar", _sqrtv, waits=[tk_stats])

            def _rstd(eng, j=j):
                eng.reciprocal(t_stat[0:1, ch:2 * ch], t_stat[0:1, 2 * ch:3 * ch])
                if j == 0:
                    eng.reciprocal(t_hst[0:1, 3:6], t_hst[0:1, 6:9])
            tk_rstd = P.step("vector", _rstd, waits=[tk_sqrt])

            def _reps(eng, j=j):
                eng.matmul(pR[0][0:128, 0:ch], t_ones1[:], t_stat[0:1, 0:ch], start=True, stop=True)
                r = eng.matmul(pR[1][0:128, 0:ch], t_ones1[:], t_stat[0:1, ch:2 * ch],
                               start=True, stop=True)
                if j == 0:
                    eng.matmul(pP[0][0:128, 0:3], t_ones1[:], t_hst[0:1, 0:3], start=True, stop=True)
                    r = eng.matmul(pP[1][0:128, 0:3], t_ones1[:], t_hst[0:1, 3:6],
                                   start=True, stop=True)
                return r
            tk_reps = P.step("tensor", _reps, waits=[tk_rstd], attach=True)

            def _happly(eng, j=j):
                for g in range(DG):
                    eng.tensor_tensor(t_h[g][:, 3:3 + ch], t_x[g][:, 3:3 + ch],
                                      pR[0][0:128, 0:ch], ALU.subtract)
                eng.drain()
                for g in range(DG):
                    eng.tensor_tensor(t_sq[g][:, 0:ch], t_h[g][:, 3:3 + ch],
                                      pR[1][0:128, 0:ch], ALU.mult)
                eng.drain()
                for g in range(DG):
                    eng.tensor_scalar(t_h[g][:, 3:3 + ch], t_sq[g][:, 0:ch],
                                      t_ca1[:, g:g + 1], t_mod[:, g:g + 1], ALU.mult, ALU.add)
                if j == 0:
                    for g in range(DG):
                        eng.tensor_tensor(t_h[g][:, 0:3], t_x[g][:, 0:3],
                                          pP[0][0:128, 0:3], ALU.subtract)
                    eng.drain()
                    for g in range(DG):
                        eng.tensor_tensor(t_sq[g][:, ch:ch + 3], t_h[g][:, 0:3],
                                          pP[1][0:128, 0:3], ALU.mult)
                    eng.drain()
                    for g in range(DG):
                        eng.tensor_scalar(t_sq[g][:, ch:ch + 3], t_sq[g][:, ch:ch + 3],
                                          t_ca1[:, g:g + 1], t_mod[:, g:g + 1], ALU.mult, ALU.add)
                    eng.drain()
                    for g in range(DG):
                        eng.tensor_tensor(t_h[g][:, 0:3], t_sq[g][:, ch:ch + 3], t_hmask[:],
                                          ALU.mult)
            tk_h = P.step("vector", _happly, waits=[tk_reps])
            prev_h = tk_h

            zev_base = P.vals["scalar"]

            def _zmm(eng, bump, j=j, base=j * ZG * 8, zb=zev_base):
                for mz in range(ZG):
                    zsz = _zsz(mz)
                    if mz >= 2:
                        eng.wait_ge(P.sems["scalar"], zb + mz - 1)
                    for k in range(8):
                        i = base + mz * 8 + k
                        li = mz * 8 + k
                        pk = j * 34 + li // 4
                        sl = t_win[pk % 4][:, 128 * (li % 4):128 * (li % 4) + zsz]
                        eng.wait_ge(P.dma_sem("wn%d" % (pk % 4)), 16 * (pk // 4 + 1))
                        r = eng.matmul(pZ[mz % 2][0:zsz, 0:ch], sl,
                                       t_h[k][:, 3:3 + ch], start=(k == 0), stop=(k == 7))
                        if j == 0:
                            if k == 0 and mz >= 1:
                                pass
                            r = eng.matmul(pS1[0:zsz, 4:7], t_win[i % 8][:, 0:zsz],
                                           t_h[k][:, 0:3], start=(k == 0), stop=(k == 7))
                        r.then_inc(P.csem("win"), 1)
                    tk_z_all[(j, mz)] = Tok(P.csem("win"), base + mz * 8 + 8)
            zmm_waits = [tk_h, tk_zev.get(j - 1)]
            if j == 0:
                def _zmm0(eng, bump, j=j, base=0, zb=zev_base):
                    for mz in range(ZG):
                        zsz = _zsz(mz)
                        if mz >= 1:
                            eng.wait_ge(P.sems["scalar"], zb + mz)
                        for k in range(8):
                            li = mz * 8 + k
                            pk = li // 4
                            sl = t_win[pk % 4][:, 128 * (li % 4):128 * (li % 4) + zsz]
                            eng.wait_ge(P.dma_sem("wn%d" % (pk % 4)), 16 * (pk // 4 + 1))
                            r = eng.matmul(pZ[mz % 2][0:zsz, 0:ch], sl,
                                           t_h[k][:, 3:3 + ch], start=(k == 0), stop=(k == 7))
                            r = eng.matmul(pS1[0:zsz, 4:7], sl,
                                           t_h[k][:, 0:3], start=(k == 0), stop=(k == 7))
                            r.then_inc(P.csem("win"), 1)
                        tk_z_all[(j, mz)] = Tok(P.csem("win"), mz * 8 + 8)
                P.raw("tensor", _zmm0, waits=zmm_waits)
            else:
                P.raw("tensor", _zmm, waits=zmm_waits)
            tk_z_all[j] = Tok(P.csem("win"), ZG * 8 * (j + 1))

            def _zev_raw(eng, bump, j=j):
                for mz in range(ZG):
                    zsz = _zsz(mz)
                    t = tk_z_all[(j, mz)]
                    eng.wait_ge(t.sem, t.val)
                    r = eng.activation(t_z[mz][0:zsz, 3:3 + ch], pZ[mz % 2][0:zsz, 0:ch],
                                       ACT.Copy)
                    if j == 0:
                        r = eng.activation(t_z[mz][0:zsz, 0:3], pS1[0:zsz, 4:7], ACT.Copy)
                    r.then_inc(P.sems["scalar"], 1)
                    bump()
            P.raw("scalar", _zev_raw, waits=[prev_conv, prev_stash])
            tk_zev[j] = Tok(P.sems["scalar"], P.vals["scalar"])

            zev0 = tk_zev[j].val - ZG
            tk_convp = {}
            for mz0 in [8, 10, 12, 14, 16, 0, 2, 4, 6]:
                pair = [mz0] + ([mz0 + 1] if mz0 + 1 < ZG else [])

                def _convp(eng, pair=pair, mz0=mz0, j=j):
                    if j > 0:
                        for mz in pair:
                            eng.tensor_copy(t_z[mz][0:_zsz(mz), 0:3],
                                            t_zst[0:_zsz(mz), 3 * mz:3 * mz + 3])
                        eng.drain()
                    for mz in pair:
                        eng.tensor_copy(t_zst[0:_zsz(mz), 3 * mz:3 * mz + 3],
                                        t_z[mz][0:_zsz(mz), ch:ch + 3])
                    eng.drain()
                    s0 = {mz0: t_e, mz0 + 1: t_ta[1]}
                    s1 = {mz0: t_f, mz0 + 1: t_ta[0]}
                    for mz in pair:
                        eng.tensor_scalar(s0[mz][0:_zsz(mz), 0:ch], t_z[mz][0:_zsz(mz), 0:ch],
                                          t_wcv[0:_zsz(mz), 0 * ZG + mz:0 * ZG + mz + 1],
                                          t_bcv[0:_zsz(mz), mz:mz + 1], ALU.mult, ALU.add)
                    eng.drain()
                    for mz in pair:
                        eng.scalar_tensor_tensor(s1[mz][0:_zsz(mz), 0:ch],
                                                 t_z[mz][0:_zsz(mz), 1:1 + ch],
                                                 t_wcv[0:_zsz(mz), 1 * ZG + mz:1 * ZG + mz + 1],
                                                 s0[mz][0:_zsz(mz), 0:ch], ALU.mult, ALU.add)
                    eng.drain()
                    for mz in pair:
                        eng.scalar_tensor_tensor(s0[mz][0:_zsz(mz), 0:ch],
                                                 t_z[mz][0:_zsz(mz), 2:2 + ch],
                                                 t_wcv[0:_zsz(mz), 2 * ZG + mz:2 * ZG + mz + 1],
                                                 s1[mz][0:_zsz(mz), 0:ch], ALU.mult, ALU.add)
                    eng.drain()
                    for mz in pair:
                        eng.scalar_tensor_tensor(t_cz[mz][0:_zsz(mz), 0:ch],
                                                 t_z[mz][0:_zsz(mz), 3:3 + ch],
                                                 t_wcv[0:_zsz(mz), 3 * ZG + mz:3 * ZG + mz + 1],
                                                 s0[mz][0:_zsz(mz), 0:ch], ALU.mult, ALU.add)
                w = ([Tok(P.sems["scalar"], zev0 + pair[-1] + 1)] if mz0 >= 8
                     else [tk_zev[j]])
                tk_convp[mz0] = P.step("vector", _convp, waits=w)
            tk_conv = tk_convp[6]
            prev_conv = tk_conv
            prev_stash = tk_conv

            def _gsilu(eng):
                for g in range(8):
                    eng.activation(t_gate[g][:, 0:ch], t_cz[8 + g][:, 0:ch], ACT.Silu)
            tk_gs = P.step("scalar", _gsilu, waits=[tk_convp[14], tk_gspill.get(j - 1)])
            tk_gs_last[0] = tk_gspill.get(j - 1)
            tk_gspill[j] = P.dma("gpsimd", "gsp",
                                 [(o_gate[g * 128:(g + 1) * 128, j * ch:(j + 1) * ch],
                                   t_gate[g][:, 0:ch]) for g in range(8)], waits=[tk_gs])
            if "cz" in dbg:
                P.dma("gpsimd", "dbg",
                      [(dbg["cz"][mz * 128:mz * 128 + _zsz(mz), j * ch:(j + 1) * ch],
                        t_cz[mz][0:_zsz(mz), 0:ch]) for mz in range(ZG)], waits=[tk_conv])
            if "h" in dbg:
                P.dma("gpsimd", "dbg",
                      [(dbg["h"][g * 128:(g + 1) * 128, j * ch:(j + 1) * ch],
                        t_h[g][:, 3:3 + ch]) for g in range(DG)],
                      waits=[tk_h, tk_z_all[j]])

            def _pw_v(eng, j=j):
                eng.tensor_scalar(t_db[:], t_tb[:, 0:1], float(j * ch), None, ALU.add)
                eng.drain()
                eng.tensor_tensor(t_db2[:], t_nsp[:], t_db[:], ALU.mult)
            tk_pwv = P.step("vector", _pw_v, waits=[tk_convp[16], prev_pw])

            def _tw_s(eng):
                eng.activation(t_tw[0:32, 0:ch], t_iota[:], ACT.Exp,
                               scale=t_nsp[0:32, 0:1], bias=t_db2[:])
            tk_tws = P.step("scalar", _tw_s, waits=[tk_pwv])

            def _p_v(eng):
                eng.tensor_scalar(t_pwb[0:32, 0:ch], t_cz[16][0:32, 0:ch],
                                  t_score[:], 20.0, ALU.mult, ALU.min)
                eng.drain()
                eng.tensor_scalar(t_pwa[0:32, 0:ch], t_pwb[0:32, 0:ch], -20.0, None, ALU.max)
            tk_pv = P.step("vector", _p_v, waits=[tk_convp[16], tk_tws])

            def _p_s(eng):
                eng.activation(t_pwb[0:32, 0:ch], t_pwa[0:32, 0:ch], ACT.Exp)
            tk_ps = P.step("scalar", _p_s, waits=[tk_pv])

            def _pw_fin(eng):
                eng.tensor_tensor(t_pw[0:32, 0:ch], t_pwb[0:32, 0:ch], t_tw[0:32, 0:ch], ALU.mult)
                eng.drain()
                eng.tensor_tensor_scan(t_dencs[0:32, 0:ch], t_pw[0:32, 0:ch], t_pw[0:32, 0:ch],
                                       t_dcar[:], ALU.add, ALU.bypass)
                eng.drain()
                eng.tensor_copy(t_dcar[:], t_dencs[0:32, ch - 1:ch])
            tk_pw_t = P.step("vector", _pw_fin, waits=[tk_ps])
            prev_pw = P.dma("gpsimd", "dsp",
                            [(o_den[0:32, j * ch:(j + 1) * ch], t_dencs[0:32, 0:ch])],
                            waits=[tk_pw_t])
            if "pw" in dbg:
                P.dma("gpsimd", "dbg", [(dbg["pw"][0:32, j * ch:(j + 1) * ch],
                                         t_pw[0:32, 0:ch])], waits=[tk_pw_t])

            tk_S = {}
            tk_B = {}

            def emit_prep(g, j=j):
                def _pr(eng, g=g):
                    return eng.matmul(pP[g % 2][0:128, 0:ch],
                                      t_r32[0:32, g * 128:(g + 1) * 128],
                                      t_pw[0:32, 0:ch], start=True, stop=True)
                w = [tk_pw_t]
                if j == 0 and g < 2:
                    w.append(tk_h)
                else:
                    w.append(tk_A.get((g - 2) % 8))
                return P.step("tensor", _pr, waits=w, attach=True)

            def emit_A(g, tk_prep_g, j=j):
                def _A(eng, g=g):
                    xv = t_cz[g][0:128, 0:ch]
                    for m in range(MTH):
                        gm = g * 4 + m
                        eng.tensor_scalar(t_ta[m][:, 0:ch], xv, t_invP[:, gm:gm + 1], None,
                                          ALU.mult)
                    eng.tensor_tensor(t_e[:, 0:ch], xv, xv, ALU.mult)
                    eng.drain()
                    eng.tensor_copy(t_ki[:, 0:ch], t_ta[0][:, 0:ch])
                    eng.tensor_copy(t_ki2[:, 0:ch], t_ta[1][:, 0:ch])
                    eng.tensor_scalar(t_f[:, 0:ch], t_e[:, 0:ch], t_wpoly[:, 2:3], None, ALU.mult)
                    eng.drain()
                    eng.scalar_tensor_tensor(t_ta[0][:, 0:ch], t_ki[:, 0:ch],
                                             t_negP[:, g * 4:g * 4 + 1], xv, ALU.mult, ALU.add)
                    eng.scalar_tensor_tensor(t_ta[1][:, 0:ch], t_ki2[:, 0:ch],
                                             t_negP[:, g * 4 + 1:g * 4 + 2], xv,
                                             ALU.mult, ALU.add)
                    eng.scalar_tensor_tensor(t_e[:, 0:ch], xv, t_wpoly[:, 1:2], t_f[:, 0:ch],
                                             ALU.mult, ALU.subtract)
                    eng.drain()
                    eng.tensor_copy(t_ki[:, 0:ch], t_ta[2][:, 0:ch])
                    eng.tensor_copy(t_ki2[:, 0:ch], t_ta[3][:, 0:ch])
                    eng.tensor_scalar(t_f[:, 0:ch], t_e[:, 0:ch], t_wpoly[:, 0:1], None,
                                      ALU.add)
                    eng.drain()
                    eng.scalar_tensor_tensor(t_ta[2][:, 0:ch], t_ki[:, 0:ch],
                                             t_negP[:, g * 4 + 2:g * 4 + 3], xv,
                                             ALU.mult, ALU.add)
                    eng.scalar_tensor_tensor(t_ta[3][:, 0:ch], t_ki2[:, 0:ch],
                                             t_negP[:, g * 4 + 3:g * 4 + 4], xv,
                                             ALU.mult, ALU.add)
                    eng.tensor_tensor(t_pp[g % 2][:, 0:ch], t_f[:, 0:ch],
                                      pP[g % 2][0:128, 0:ch], ALU.mult)
                    eng.drain()
                return P.step("vector", _A, waits=[tk_convp[g & ~1], tk_prep_g, tk_S.get(g - 1)])

            def emit_S(g):
                def _S(eng, g=g):
                    for m in range(MTH):
                        gm = g * 4 + m
                        eng.activation(t_s4[m][:, 0:ch], t_ta[m][:, 0:ch], ACT.Sin,
                                       scale=t_th4[:, gm:gm + 1])
                        eng.activation(t_s8[m][:, 0:ch], t_ta[m][:, 0:ch], ACT.Sin,
                                       scale=t_th4[:, gm:gm + 1], bias=t_pih[:])
                    eng.drain()
                    for m in range(MTH):
                        eng.activation(t_mg[2 * m + 1][:, 0:ch], t_s4[m][:, 0:ch],
                                       ACT.Square)
                return P.step("scalar", _S, waits=[tk_A[g], tk_B.get(g - 1)])

            def emit_B(g, j=j):
                def _B(eng, g=g):
                    PPg = t_pp[g % 2]
                    for m in range(MTH):
                        eng.tensor_scalar(t_cs[2 * m][:, 0:ch], t_mg[2 * m + 1][:, 0:ch],
                                          -2.0, 1.0, ALU.mult, ALU.add)
                        eng.tensor_tensor(t_mg[2 * m][:, 0:ch], t_s4[m][:, 0:ch],
                                          t_s8[m][:, 0:ch], ALU.mult)
                    eng.drain()
                    for m in range(MTH):
                        eng.tensor_tensor(t_s4[m][:, 0:ch], t_mg[2 * m][:, 0:ch],
                                          t_cs[2 * m][:, 0:ch], ALU.mult)
                        eng.tensor_tensor(t_s8[m][:, 0:ch], t_mg[2 * m][:, 0:ch],
                                          t_mg[2 * m][:, 0:ch], ALU.mult)
                    eng.drain()
                    for m in range(MTH):
                        eng.scalar_tensor_tensor(t_cs[2 * m][:, 0:ch], t_s8[m][:, 0:ch],
                                                 -8.0, PPg[:, 0:ch], ALU.mult, ALU.mult)
                    eng.drain()
                    for m in range(MTH):
                        eng.tensor_tensor(t_mg[2 * m][:, 0:ch], PPg[:, 0:ch],
                                          t_cs[2 * m][:, 0:ch], ALU.add)
                        eng.scalar_tensor_tensor(t_mg[2 * m + 1][:, 0:ch], t_s4[m][:, 0:ch],
                                                 4.0, PPg[:, 0:ch], ALU.mult, ALU.mult)
                    eng.drain()
                    for m in range(MTH):
                        for side in range(2):
                            i = 2 * m + side
                            col = _scol(g, m, side)
                            eng.tensor_tensor_scan(t_cs[i][:, 0:ch], t_mg[i][:, 0:ch],
                                                   t_mg[i][:, 0:ch], t_carry[:, col:col + 1],
                                                   ALU.add, ALU.bypass)
                    eng.drain()
                    for m in range(MTH):
                        for side in range(2):
                            i = 2 * m + side
                            col = _scol(g, m, side)
                            eng.tensor_copy(t_carry[:, col:col + 1], t_cs[i][:, ch - 1:ch])
                    eng.drain()
                wl = [tk_S[g]]
                if g > 0:
                    wl.append(tk_spill.get((j, g - 1)))
                    wl.append(tk_spill.get((j, g - 2), last_spill.get(j - 1)))
                else:
                    wl.append(last_spill.get(j - 1))
                    wl.append(tk_spill.get((j - 1, 6)))
                return P.step("vector", _B, waits=wl)

            def emit_spill(g, j=j):
                trs = []
                for m in range(MTH):
                    for side in range(2):
                        i = 2 * m + side
                        rb = _scol(g, m, side) * 128
                        trs.append((o_cs[rb:rb + 128, j * ch:(j + 1) * ch], t_cs[i][:, 0:ch]))
                tk_spill[(j, g)] = P.dma("gpsimd", "sp%d" % (g % 2), trs, waits=[tk_B[g]])
                if "mg" in dbg:
                    trs2 = []
                    for m in range(MTH):
                        for side in range(2):
                            i = 2 * m + side
                            rb = _scol(g, m, side) * 128
                            trs2.append((dbg["mg"][rb:rb + 128, j * ch:(j + 1) * ch],
                                         t_mg[i][:, 0:ch]))
                    P.dma("gpsimd", "dbg", trs2, waits=[tk_B[g]])

            for g in range(8):
                pr = emit_prep(g)
                tk_A[g] = emit_A(g, pr)
                if g >= 1:
                    tk_B[g - 1] = emit_B(g - 1)
                    emit_spill(g - 1)
                tk_S[g] = emit_S(g)
            tk_B[7] = emit_B(7)
            emit_spill(7)
            last_spill[j] = tk_spill[(j, 7)]
            prev_B7 = tk_B[7]

        def _sout(eng):
            eng.memzero(t_sfin[:, 0:66])
            eng.drain()
            eng.tensor_copy(t_sfin[:, 0:NS], t_carry[:])
            eng.drain()
            eng.tensor_copy(t_sfin[0:32, 64:65], t_dcar[:])
        tk_fin = P.step("vector", _sout, waits=[prev_B7, prev_pw])
        if F is None:
            P.dma("sync", "store", [(o_s, t_sfin[:, 0:66])], waits=[tk_fin])

            def _bar(eng):
                for s in ("sp0", "sp1", "gsp", "dsp", "dbg", "store"):
                    if s in P.dsems:
                        eng.wait_ge(P.dma_sem(s), P.dvals[s])
            P.step("sync", _bar)
        else:
            def _slots(eng):
                for i in range(NCORE):
                    eng.tensor_scalar(t_sall8[:, 66 * i:66 * (i + 1)], t_sfin[:, 0:66],
                                      t_wsl[:, i:i + 1], None, ALU.mult)
            tk_slots = P.step("vector", _slots, waits=[tk_fin])
            P.dma("gpsimd", "ccin",
                  [(F["cc_in"].ap()[i * 128:(i + 1) * 128, 0:66],
                    t_sall8[:, 66 * i:66 * (i + 1)]) for i in range(NCORE)]
                  ,
                  waits=[tk_slots])

            def _barg(eng):
                for s in ("sp0", "sp1", "gsp", "dsp", "dbg", "x", "ccin"):
                    if s in P.dsems:
                        eng.wait_ge(P.dma_sem(s), P.dvals[s])
            bt = [P.step("gpsimd", _barg)]

            def _bars(eng):
                for s in P.dvals:
                    if s.startswith(("wn", "wm", "mc", "fesmall")):
                        eng.wait_ge(P.dma_sem(s), P.dvals[s])
            bt.append(P.step("sync", _bars))
            for e in ("tensor", "vector", "scalar"):
                bt.append(P.step(e, lambda eng: None))
            F["fe_bar"] = bt

    return nc



def build_be(T_core, ch, debug=(), F=None):
    nch = T_core // ch
    if F is None:
        nc = bass.Bass("TRN2", target_bir_lowering=False)
        i_xT = nc.dram_tensor("xT", [D, T_core + 3], F32, kind="ExternalInput").ap()
        i_cs = nc.dram_tensor("sp_cs", [NS * 128, T_core], F32, kind="ExternalInput").ap()
        i_gate = nc.dram_tensor("sp_gate", [8 * 128, T_core], F32, kind="ExternalInput").ap()
        i_den = nc.dram_tensor("sp_den", [32, T_core], F32, kind="ExternalInput").ap()
        i_sall = nc.dram_tensor("s_all", [NCORE * 128, 66], F32, kind="ExternalInput").ap()
        i_modrow = nc.dram_tensor("mod_row", [1, 6 * D], F32, kind="ExternalInput").ap()
    else:
        nc = F["nc"]
        i_xT = F["i_xT"]
        i_cs = F["sp_cs"]
        i_gate = F["sp_gate"]
        i_den = F["sp_den"]
        i_sall = F["cc_out"].ap()
    i_smask = nc.dram_tensor("smask", [128, NCORE], F32, kind="ExternalInput").ap()
    i_w1 = nc.dram_tensor("w1t", [FG * 8 * 128, 128], F32, kind="ExternalInput").ap()
    i_w2 = nc.dram_tensor("w2t", [8 * FG * 128, 128], F32, kind="ExternalInput").ap()
    i_wo = nc.dram_tensor("wot", [8 * 8 * 128, 128], F32, kind="ExternalInput").ap()
    i_wre = nc.dram_tensor("wrebd", [8 * 128, 128], F32, kind="ExternalInput").ap()
    i_nsc = nc.dram_tensor("nsccols", [128, 8], F32, kind="ExternalInput").ap()
    i_obd = nc.dram_tensor("obd32", [128, 8 * 32], F32, kind="ExternalInput").ap()
    i_b1 = nc.dram_tensor("b1c", [128, FG], F32, kind="ExternalInput").ap()
    i_b2 = nc.dram_tensor("b2c", [128, 8], F32, kind="ExternalInput").ap()
    if F is None:
        i_r32 = nc.dram_tensor("r32", [32, 8 * 128], F32, kind="ExternalInput").ap()
        i_ones1 = nc.dram_tensor("ones_k1", [1, 128], F32, kind="ExternalInput").ap()
        i_onesc = nc.dram_tensor("ones_col", [128, 1], F32, kind="ExternalInput").ap()
        i_cst = nc.dram_tensor("cst", [128, NCST], F32, kind="ExternalInput").ap()

    o_y = nc.dram_tensor("yT", [D, T_core], F32, kind="ExternalOutput").ap()
    dbg_shapes = {"y": [8 * 128, T_core], "att": [8 * 128, T_core], "x1": [DG * 128, T_core],
                  "h2": [DG * 128, T_core], "ffg": [FG * 128, T_core], "sk": [32, T_core],
                  "inv": [32, T_core], "scar": [128, 66], "ycar": [128, 8],
                  "scr2": [128, 66], "smk": [128, NCORE], "scar0": [128, 66]}
    dbg = {nm: nc.dram_tensor("dbg_" + nm, dbg_shapes[nm], F32, kind="ExternalOutput").ap()
           for nm in debug}

    es = contextlib.ExitStack()
    with es:
        def sb(name, shape, dt=F32):
            if F is not None:
                name = "b_" + name
            return es.enter_context(nc.sbuf_tensor(name, shape, dt))

        def pt(name):
            if F is not None:
                name = "b_" + name
            return es.enter_context(nc.psum_tensor(name, [128, 512], F32))

        t_mod = sb("t_mod", [128, 48]) if F is None else F["t_mod"]
        t_cb1 = sb("t_cb1", [128, 8])
        t_ga = sb("t_ga", [128, 8])
        t_gb256 = sb("t_gb256", [128, 8])
        t_gbfix = sb("t_gbfix", [128, 8])
        t_scar = sb("t_scar", [128, 66])
        t_scr2 = sb("t_scr2", [128, 66])
        t_smk = sb("t_smk", [128, NCORE])
        t_dcar = sb("t_dcar", [32, 1])
        t_ycar = sb("t_ycar", [128, 8])
        t_wre = [sb(f"t_wre{i}", [128, 128]) for i in range(8)]
        t_nsc = sb("t_nsc", [128, 8])
        t_obd = sb("t_obd", [128, 8 * 32])
        t_b1 = sb("t_b1", [128, FG])
        t_bias1 = sb("t_bias1", [128, FG])
        t_b2p = sb("t_b2p", [128, 8])
        t_bias2 = sb("t_bias2", [128, 8])
        if F is None:
            t_r32 = sb("t_r32", [32, 8 * 128])
            t_ones1 = sb("t_ones1", [1, 128])
            t_onesc = sb("t_onesc", [128, 1])
            t_cst = sb("t_cst", [128, NCST])
            t_modrow = sb("t_modrow", [1, 6 * D])
        else:
            t_r32 = F["t_r32"]
            t_ones1 = F["t_ones1"]
            t_onesc = F["t_onesc"]
            t_cst = F["t_cst"]
        t_stat = sb("t_stat", [1, 4 * ch])

        t_x = [sb(f"t_x{g}", [128, ch]) for g in range(DG)]
        t_csr = [sb(f"t_csr{i}", [128, ch]) for i in range(16)]
        t_sqg = [sb(f"t_sqg{i}", [128, ch]) for i in range(8)]
        t_y = [sb(f"t_y{g}", [128, ch]) for g in range(8)]
        t_gt = [sb(f"t_gt{g}", [128, ch]) for g in range(8)]
        t_x1 = [sb(f"t_x1{g}", [128, ch]) for g in range(DG)]
        t_gfp = t_csr + t_sqg + t_gt
        t_den = sb("t_den", [32, ch])
        t_inv = sb("t_inv", [32, ch])
        t_sk = sb("t_sk", [32, ch])
        t_ssc = sb("t_ssc", [32, ch])
        t_e = sb("t_e", [128, ch])
        t_f = sb("t_f", [128, ch])
        t_g2 = sb("t_g2", [128, ch])
        t_gg = sb("t_gg", [128, ch])
        t_h3 = sb("t_h3", [128, ch])
        t_n1 = sb("t_n1", [128, ch], I32)
        t_m2i = sb("t_m2i", [128, ch], I32)
        t_gq = sb("t_gq", [128, ch], I32)
        _lfp = ([t_e, t_f, t_g2, t_gg, t_h3]
                + [sb(f"t_gl{i}", [128, ch]) for i in range(11)])
        _li32 = [t_n1, t_m2i, t_gq, sb("t_gl_i", [128, ch], I32)]
        GL = [dict(n1f=_lfp[4 * L + 0], e=_lfp[4 * L + 1], f=_lfp[4 * L + 2],
                   gg=_lfp[4 * L + 3], tmp=_li32[L])
              for L in range(4)]
        t_wb1 = [sb(f"t_wb1_{i}", [128, 512]) for i in range(4)]
        t_wb2 = [sb(f"t_wb2_{i}", [128, 512]) for i in range(4)]
        t_wbo = [sb(f"t_wbo_{i}", [128, 512]) for i in range(4)]

        pA = [pt("pA0"), pt("pA1")]
        pB = [pt("pB0"), pt("pB1")]
        pSS = pt("pSS")
        pS1 = pt("pS1")
        pS2 = pt("pS2")
        pRp = pt("pRp")

        if F is None:
            block = es.enter_context(nc.Block())
            P = Prog(nc, block, es)
            bar = []
        else:
            block = F["block"]
            P = F["P"]
            bar = F["fe_bar"]
            for e in Prog.ENGINES:
                P.step(e, lambda eng: None, waits=bar)

        ld0 = [
            (t_smk[:], i_smask),
            (t_nsc[:], i_nsc), (t_obd[:], i_obd), (t_b1[:], i_b1), (t_b2p[:], i_b2),
        ] + [(t_wre[i][:], i_wre[i * 128:(i + 1) * 128, :]) for i in range(8)]
        if F is None:
            ld0 += [(t_r32[:], i_r32), (t_ones1[:], i_ones1), (t_onesc[:], i_onesc),
                    (t_cst[:], i_cst), (t_modrow[:], i_modrow)]
        tk_small = P.dma("sync", "besmall", ld0,
                         waits=[F["cc_done"]] if F is not None else [])
        tk_s0 = P.dma("gpsimd" if F is not None else "sync", "scl",
                      [(t_scr2[:], i_sall[0:128, 0:66])],
                      waits=[F["cc_done"]] if F is not None else [])

        if F is None:
            mod_tr = []
            for v6 in range(6):
                for k in range(8):
                    mod_tr.append((t_mod[:, v6 * 8 + k:v6 * 8 + k + 1],
                                   t_modrow[0:1, v6 * D + k * 128:v6 * D + (k + 1) * 128],
                                   dict(allow_slow_non_contiguous=True)))
            tk_modc = P.dma("sync", "besmall", mod_tr, waits=[tk_small])
        else:
            tk_modc = tk_small

        if "scr2" in dbg:
            P.dma("gpsimd", "dbg", [(dbg["scr2"], t_scr2[:])], waits=[tk_small])
        if "smk" in dbg:
            P.dma("gpsimd", "dbg", [(dbg["smk"], t_smk[:])], waits=[tk_small])

        def _scar0(eng):
            if "scr2" in dbg or "smk" in dbg:
                eng.wait_ge(P.dma_sem("dbg"), P.dvals["dbg"])
            eng.tensor_scalar(t_scar[:], t_scr2[:], t_smk[:, 0:1], None, ALU.mult)
        tk_sc = P.step("vector", _scar0, waits=[tk_small, tk_s0])
        if "scar0" in dbg:
            P.dma("gpsimd", "dbg2", [(dbg["scar0"], t_scar[:])], waits=[tk_sc])
            def _w0(eng):
                eng.wait_ge(P.dma_sem("dbg2"), P.dvals["dbg2"])
            tk_sc = P.step("vector", _w0, waits=[tk_sc])
        for i in range(1, NCORE):
            tk_l = P.dma("gpsimd" if F is not None else "sync", "scl",
                         [(t_scr2[:], i_sall[i * 128:(i + 1) * 128, 0:66])],
                         waits=[tk_sc])

            def _sacc(eng, i=i):
                eng.scalar_tensor_tensor(t_scar[:], t_scr2[:], t_smk[:, i:i + 1], t_scar[:],
                                         ALU.mult, ALU.add)
            tk_sc = P.step("vector", _sacc, waits=[tk_l])

        def _postcar(eng):
            eng.tensor_copy(t_dcar[:], t_scar[0:32, 64:65])
            eng.tensor_scalar(t_cb1[:], t_mod[:, 32:40], 1.0, None, ALU.add)
            for src, dst in ((16, t_ga), (40, t_gbfix)):
                eng.tensor_scalar(t_e[:, 0:8], t_mod[:, src:src + 8], 0.0, None, ALU.is_ge)
                eng.drain()
                eng.tensor_scalar(t_f[:, 0:8], t_e[:, 0:8], 2.0, -1.0, ALU.mult, ALU.add)
                eng.tensor_scalar(t_e[:, 8:16], t_mod[:, src:src + 8], -1.0, None, ALU.mult)
                eng.drain()
                eng.tensor_tensor(t_e[:, 0:8], t_e[:, 8:16], t_mod[:, src:src + 8], ALU.max)
                eng.drain()
                eng.tensor_scalar(t_e[:, 0:8], t_e[:, 0:8], 0.25, None, ALU.max)
                eng.drain()
                eng.tensor_tensor(dst[:], t_f[:, 0:8], t_e[:, 0:8], ALU.mult)
                eng.drain()
            eng.tensor_scalar(t_gb256[:], t_gbfix[:], 1.0 / 256.0, None, ALU.mult)
            eng.drain()
            eng.tensor_scalar(t_gbfix[:], t_gbfix[:], -128.0, None, ALU.mult)
            eng.tensor_scalar(t_bias1[:], t_b1[:], 256.0, OFF15 + 0.5, ALU.mult, ALU.add)
            eng.tensor_scalar(t_e[:, 0:8], t_b2p[:], 256.0, OFF15 + 0.5, ALU.mult, ALU.add)
            eng.drain()
            eng.tensor_copy(t_b2p[:], t_e[:, 0:8])
            for i in range(8):
                eng.drain()
                eng.tensor_scalar(t_wb1[0][:, 0:128], t_wre[i][:], t_nsc[:, i:i + 1],
                                  None, ALU.mult)
                eng.drain()
                eng.tensor_copy(t_wre[i][:], t_wb1[0][:, 0:128])
        tk_prep = P.step("vector", _postcar, waits=[tk_sc, tk_modc])

        def _ycar_mm(eng):
            r = None
            for g in range(8):
                for i in range(8):
                    m, side = i // 2, i % 2
                    col = _scol(g, m, side)
                    r = eng.matmul(pA[0][0:128, g:g + 1], t_wre[i][:],
                                   t_scar[:, col:col + 1],
                                   start=(g == 0 and i == 0), stop=(g == 7 and i == 7))
            return r
        tk_ycmm = P.step("tensor", _ycar_mm, waits=[tk_prep], attach=True)

        def _ycar_ev(eng):
            eng.tensor_copy(t_ycar[:], pA[0][0:128, 0:8])
        tk_ycar = P.step("vector", _ycar_ev, waits=[tk_ycmm])
        if "scar" in dbg:
            P.dma("gpsimd", "dbg", [(dbg["scar"], t_scar[:])], waits=[tk_prep])
        if "ycar" in dbg:
            P.dma("gpsimd", "dbg", [(dbg["ycar"], t_ycar[:])], waits=[tk_ycar])

        w1_cnt = [0]
        w2_cnt = [0]
        wo_cnt = [0]
        tk_w1_free = {}
        tk_w2_free = {}
        tk_wo_free = {}
        tk_store = {}
        tk_b2 = {}
        tk_bias2_done = None
        prev_sq = {}
        prev_mm2 = tk_ycar
        prev_ffB = None
        prev_den = None
        prev_h2 = None

        for j in range(nch):
            tk_x = P.dma("gpsimd", "x",
                         [(t_x[g][:, 0:ch], i_xT[g * 128:(g + 1) * 128, j * ch + 3:j * ch + 3 + ch])
                          for g in range(DG)], waits=[tk_store.get(j - 1), tk_ycar])

            tk_dl = P.dma("gpsimd", "den",
                          [(t_den[0:32, 0:ch], i_den[0:32, j * ch:(j + 1) * ch])],
                          waits=[prev_den])

            def _den(eng):
                eng.tensor_scalar(t_e[0:32, 0:ch], t_den[0:32, 0:ch], t_dcar[:], 1e-4,
                                  ALU.add, ALU.max)
                eng.drain()
                eng.reciprocal(t_inv[0:32, 0:ch], t_e[0:32, 0:ch])
            tk_inv = P.step("vector", _den, waits=[tk_dl, tk_prep, tk_ycar])
            prev_den = tk_inv
            if "inv" in dbg:
                P.dma("gpsimd", "dbg", [(dbg["inv"][0:32, j * ch:(j + 1) * ch],
                                           t_inv[0:32, 0:ch])], waits=[tk_inv])

            tk_yev = {}
            tk_ssg = {}
            tk_csr = {}
            tk_ymm = {}
            for g in range(8):
                cset = t_csr[8 * (g % 2):8 * (g % 2) + 8]
                trs = []
                for m in range(MTH):
                    for side in range(2):
                        i = 2 * m + side
                        rb = _scol(g, m, side) * 128
                        trs.append((cset[i][:, 0:ch], i_cs[rb:rb + 128, j * ch:(j + 1) * ch]))
                pw_ = ([prev_sq.get((j, g - 2)), tk_ymm.get((j, g - 2))] if g >= 2 else
                       [prev_sq.get((j - 1, g + 6)), tk_ymm.get((j - 1, g + 6))])
                tk_csr[g] = P.dma("sync", "cs%d" % (g % 2), trs,
                                  waits=pw_ + [prev_mm2, tk_small])

                def _sqs(eng, g=g, cset=cset):
                    for i in range(8):
                        m, side = i // 2, i % 2
                        col = _scol(g, m, side)
                        eng.activation(t_sqg[i][:, 0:ch], cset[i][:, 0:ch], ACT.Square,
                                       bias=t_scar[:, col:col + 1])
                tk_sq = P.step("scalar", _sqs,
                               waits=[tk_csr[g], tk_ssg.get(g - 1), prev_ffB])
                prev_sq[(j, g)] = tk_sq

                def _ssmm(eng, g=g):
                    r = None
                    for i in range(8):
                        r = eng.matmul(pSS[0:32, 0:ch], t_obd[:, g * 32:(g + 1) * 32],
                                       t_sqg[i][:, 0:ch], start=(g == 0 and i == 0),
                                       stop=(g == 7 and i == 7))
                    return r
                tk_ssg[g] = P.step("tensor", _ssmm, waits=[tk_sq, tk_inv if g == 0 else None],
                                   attach=True)

                def _ymm(eng, g=g, cset=cset):
                    r = None
                    for i in range(8):
                        r = eng.matmul(pA[g % 2][0:128, 0:ch], t_wre[i][:], cset[i][:, 0:ch],
                                       start=(i == 0), stop=(i == 7))
                    return r
                tk_ym = P.step("tensor", _ymm,
                               waits=[tk_csr[g], tk_yev.get(g - 2, tk_ycar if g < 2 else None)],
                               attach=True)
                tk_ymm[(j, g)] = tk_ym

                def _yev(eng, g=g):
                    eng.activation(t_y[g][:, 0:ch], pA[g % 2][0:128, 0:ch], ACT.Copy)
                tk_yev[g] = P.step("scalar", _yev, waits=[tk_ym, prev_h2])

            def _sscale(eng):
                eng.tensor_tensor(t_ssc[0:32, 0:ch], t_inv[0:32, 0:ch], t_inv[0:32, 0:ch],
                                  ALU.mult)
                eng.drain()
                eng.scalar_tensor_tensor(t_sk[0:32, 0:ch], t_ssc[0:32, 0:ch], 1.0 / 256.0,
                                         pSS[0:32, 0:ch], ALU.mult, ALU.mult)
            tk_ms = P.step("vector", _sscale, waits=[tk_ssg[7], tk_inv])

            def _srt(eng):
                eng.activation(t_ssc[0:32, 0:ch], t_sk[0:32, 0:ch], ACT.Sqrt,
                               bias=t_cst[0:32, C_EPSRMS:C_EPSRMS + 1])
            tk_srt = P.step("scalar", _srt, waits=[tk_ms])

            def _skv(eng):
                eng.reciprocal(t_sk[0:32, 0:ch], t_ssc[0:32, 0:ch])
                eng.drain()
                eng.tensor_tensor(t_ssc[0:32, 0:ch], t_inv[0:32, 0:ch], t_sk[0:32, 0:ch],
                                  ALU.mult)
            tk_sk = P.step("vector", _skv, waits=[tk_srt])
            if "sk" in dbg:
                P.dma("gpsimd", "dbg", [(dbg["sk"][0:32, j * ch:(j + 1) * ch],
                                           t_ssc[0:32, 0:ch])], waits=[tk_sk])

            tk_gl = P.dma("gpsimd", "gate",
                          [(t_gt[g][:, 0:ch], i_gate[g * 128:(g + 1) * 128, j * ch:(j + 1) * ch])
                           for g in range(8)], waits=[prev_mm2, tk_ycar])

            tk_att = {}
            for g0 in range(0, 8, 2):
                tk_srp = {}
                for g in (g0, g0 + 1):
                    def _srep(eng, g=g):
                        return eng.matmul(pB[g % 2][0:128, 0:ch],
                                          t_r32[0:32, g * 128:(g + 1) * 128],
                                          t_ssc[0:32, 0:ch], start=True, stop=True)
                    tk_srp[g] = P.step("tensor", _srep,
                                       waits=[tk_sk,
                                              tk_att.get(g - 2,
                                                         prev_mm2 if g < 2 else None)],
                                       attach=True)

                def _att(eng, g0=g0):
                    for g in (g0, g0 + 1):
                        eng.tensor_tensor(t_e[:, 0:ch] if g == g0 else t_f[:, 0:ch],
                                          t_gt[g][:, 0:ch], pB[g % 2][0:128, 0:ch],
                                          ALU.mult)
                    eng.drain()
                    for g in (g0, g0 + 1):
                        eng.scalar_tensor_tensor(t_gt[g][:, 0:ch], t_y[g][:, 0:ch],
                                                 t_ycar[:, g:g + 1],
                                                 t_e[:, 0:ch] if g == g0 else t_f[:, 0:ch],
                                                 ALU.add, ALU.mult)
                tk_att[g0] = tk_att[g0 + 1] = P.step(
                    "vector", _att,
                    waits=[tk_srp[g0 + 1], tk_gl, tk_yev[g0 + 1]])
            if "y" in dbg:
                P.dma("gpsimd", "dbg", [(dbg["y"][g * 128:(g + 1) * 128, j * ch:(j + 1) * ch],
                                           t_y[g][:, 0:ch]) for g in range(8)],
                      waits=[tk_yev[7]])
            if "att" in dbg:
                P.dma("gpsimd", "dbg", [(dbg["att"][g * 128:(g + 1) * 128, j * ch:(j + 1) * ch],
                                           t_gt[g][:, 0:ch]) for g in range(8)],
                      waits=[tk_att[7]])

            tk_x1 = {}
            for md in range(DG):
                base = wo_cnt[0]
                for pq in range(2):
                    pk = (base + pq * 4) // 4
                    ii = md * 8 + pq * 4 + j * 64
                    P.dma("sync", "wo%d" % (pk % 4),
                          [(t_wbo[pk % 4][:].rearrange("p (b c) -> p b c", b=4),
                            i_wo[(ii % 64) * 128:(ii % 64 + 4) * 128, :]
                            .rearrange("(b p) c -> p b c", b=4))],
                          waits=[tk_small] if pk < 4 else [Tok(P.csem("wo"), 4 * (pk - 4) + 4)])
                wo_cnt[0] += 8

                def _womm(eng, bump, md=md, base=base):
                    for k in range(8):
                        i = base + k
                        pk = i // 4
                        eng.wait_ge(P.dma_sem("wo%d" % (pk % 4)), 16 * (pk // 4 + 1))
                        r = eng.matmul(pB[md % 2][0:128, 0:ch],
                                       t_wbo[pk % 4][:, 128 * (i % 4):128 * (i % 4) + 128],
                                       t_gt[k][:, 0:ch], start=(k == 0), stop=(k == 7))
                        r.then_inc(P.csem("wo"), 1)
                wo_waits = [tk_att[7], tk_x]
                if md >= 2:
                    wo_waits.append(tk_x1[md - 2])
                else:
                    wo_waits.append(tk_att[6 + md])
                P.raw("tensor", _womm, waits=wo_waits)
                tk_womm = Tok(P.csem("wo"), base + 8)

                def _x1f(eng, md=md):
                    eng.scalar_tensor_tensor(t_x1[md][:, 0:ch], pB[md % 2][0:128, 0:ch],
                                             t_ga[:, md:md + 1], t_x[md][:, 0:ch],
                                             ALU.mult, ALU.add)
                tk_x1[md] = P.step("vector", _x1f, waits=[tk_womm])
            if "x1" in dbg:
                P.dma("gpsimd", "dbg", [(dbg["x1"][g * 128:(g + 1) * 128, j * ch:(j + 1) * ch],
                                           t_x1[g][:, 0:ch]) for g in range(DG)],
                      waits=[tk_x1[7]])

            def _sq2(eng):
                for g in range(DG):
                    eng.activation(t_sqg[g][:, 0:ch], t_x1[g][:, 0:ch], ACT.Square)
            tk_sq2 = P.step("scalar", _sq2, waits=[tk_x1[7], tk_ssg[7]])

            def _sums2(eng):
                for k in range(DG):
                    eng.matmul(pS1[0:1, 0:ch], t_onesc[:], t_x1[k][:, 0:ch],
                               start=(k == 0), stop=(k == DG - 1))
                for k in range(DG):
                    r = eng.matmul(pS2[0:1, 0:ch], t_onesc[:], t_sqg[k][:, 0:ch],
                                   start=(k == 0), stop=(k == DG - 1))
                return r
            tk_sums2 = P.step("tensor", _sums2, waits=[tk_x1[7], tk_sq2], attach=True)

            def _stats2(eng):
                eng.tensor_scalar(t_stat[0:1, 0:ch], pS1[0:1, 0:ch], 1.0 / D, None, ALU.mult)
                eng.tensor_scalar(t_stat[0:1, ch:2 * ch], pS2[0:1, 0:ch], 1.0 / D, None, ALU.mult)
                eng.drain()
                eng.tensor_tensor(t_stat[0:1, 2 * ch:3 * ch], t_stat[0:1, 0:ch],
                                  t_stat[0:1, 0:ch], ALU.mult)
                eng.drain()
                eng.tensor_tensor(t_stat[0:1, 3 * ch:4 * ch], t_stat[0:1, ch:2 * ch],
                                  t_stat[0:1, 2 * ch:3 * ch], ALU.subtract)
            tk_st2 = P.step("vector", _stats2, waits=[tk_sums2])

            def _sqrt2(eng):
                eng.activation(t_stat[0:1, 2 * ch:3 * ch], t_stat[0:1, 3 * ch:4 * ch], ACT.Sqrt,
                               bias=t_cst[0:1, C_EPSLN:C_EPSLN + 1])
            tk_sr2 = P.step("scalar", _sqrt2, waits=[tk_st2])

            def _rstd2(eng):
                eng.reciprocal(t_stat[0:1, ch:2 * ch], t_stat[0:1, 2 * ch:3 * ch])
            tk_rs2 = P.step("vector", _rstd2, waits=[tk_sr2])

            def _reps2(eng):
                eng.matmul(pRp[0:128, 0:ch], t_ones1[:], t_stat[0:1, 0:ch], start=True, stop=True)
                return eng.matmul(pSS[0:128, 0:ch], t_ones1[:], t_stat[0:1, ch:2 * ch],
                                  start=True, stop=True)
            tk_rep2 = P.step("tensor", _reps2, waits=[tk_rs2, tk_ms], attach=True)

            def _h2(eng):
                for g in range(DG):
                    eng.tensor_tensor(t_y[g][:, 0:ch], t_x1[g][:, 0:ch], pRp[0:128, 0:ch],
                                      ALU.subtract)
                eng.drain()
                for g in range(DG):
                    eng.tensor_tensor(t_sqg[g][:, 0:ch], t_y[g][:, 0:ch], pSS[0:128, 0:ch],
                                      ALU.mult)
                eng.drain()
                for g in range(DG):
                    eng.tensor_scalar(t_y[g][:, 0:ch], t_sqg[g][:, 0:ch],
                                      t_cb1[:, g:g + 1], t_mod[:, 24 + g:24 + g + 1],
                                      ALU.mult, ALU.add)
            tk_h2 = P.step("vector", _h2, waits=[tk_rep2, tk_yev[7]])
            prev_h2 = tk_h2
            if "h2" in dbg:
                P.dma("gpsimd", "dbg", [(dbg["h2"][g * 128:(g + 1) * 128, j * ch:(j + 1) * ch],
                                           t_y[g][:, 0:ch]) for g in range(DG)],
                      waits=[tk_h2])

            tk_pv1 = {}
            tk_pv1a = {}
            tk_pgs = {}
            tk_pv2 = {}
            tk_mm1p = {}
            NP = FG // 2

            def emit_pv2(p):
                A0 = GL[(p % 2) * 2 + 0]
                A1 = GL[(p % 2) * 2 + 1]

                def _pv2(eng, p=p, A0=A0, A1=A1):
                    for A in (A0, A1):
                        eng.tensor_tensor(A["n1f"][:, 0:ch], A["gg"][:, 0:ch],
                                          A["f"][:, 0:ch], ALU.subtract)
                    eng.drain()
                    for A in (A0, A1):
                        eng.tensor_scalar(A["gg"][:, 0:ch], A["f"][:, 0:ch], 256.0,
                                          OFF15 + 0.5, ALU.mult, ALU.add)
                        eng.tensor_tensor(A["e"][:, 0:ch], A["n1f"][:, 0:ch],
                                          A["tmp"][:, 0:ch], ALU.mult)
                    eng.drain()
                    for A in (A0, A1):
                        eng.scalar_tensor_tensor(A["tmp"][:, 0:ch], A["e"][:, 0:ch], 4.0,
                                                 A["gg"][:, 0:ch], ALU.mult, ALU.add)
                    eng.drain()
                    for L2, A in ((0, A0), (1, A1)):
                        eng.tensor_copy(t_gfp[2 * p + L2][:, 0:ch], A["tmp"][:, 0:ch])
                tk_pv2[p] = P.step("vector", _pv2, waits=[tk_pgs[p]])

            for p in range(NP):
                for L2 in range(2):
                    mf = 2 * p + L2
                    base = w1_cnt[0]
                    for pq in range(2):
                        pk = (base + pq * 4) // 4
                        ii = mf * 8 + pq * 4
                        P.dma("sync", "w1_%d" % (pk % 4),
                              [(t_wb1[pk % 4][:].rearrange("p (b c) -> p b c", b=4),
                                i_w1[ii * 128:(ii + 4) * 128, :]
                                .rearrange("(b p) c -> p b c", b=4))],
                              waits=[tk_small] if pk < 4
                              else [Tok(P.csem("w1"), 4 * (pk - 4) + 4)])
                    w1_cnt[0] += 8

                    def _mm1(eng, bump, mf=mf, base=base):
                        for k in range(8):
                            i = base + k
                            pk = i // 4
                            eng.wait_ge(P.dma_sem("w1_%d" % (pk % 4)), 16 * (pk // 4 + 1))
                            r = eng.matmul(pA[mf % 2][0:128, 0:ch],
                                           t_wb1[pk % 4][:, 128 * (i % 4):128 * (i % 4) + 128],
                                           t_y[k][:, 0:ch], start=(k == 0), stop=(k == 7))
                            r.then_inc(P.csem("w1"), 1)
                    mm1_waits = [tk_h2]
                    if p >= 1:
                        mm1_waits.append(tk_pv1a[p - 1])
                    P.raw("tensor", _mm1, waits=mm1_waits)
                    tk_mm1p[mf] = Tok(P.csem("w1"), base + 8)

                A0 = GL[(p % 2) * 2 + 0]
                A1 = GL[(p % 2) * 2 + 1]

                def _pv1a(eng, p=p, A0=A0, A1=A1):
                    for L2, A in ((0, A0), (1, A1)):
                        mf = 2 * p + L2
                        eng.tensor_scalar(A["tmp"][:, 0:ch], pA[mf % 2][0:128, 0:ch], 256.0,
                                          t_bias1[:, mf:mf + 1], ALU.mult, ALU.add)
                w = [tk_mm1p[2 * p], tk_mm1p[2 * p + 1]]
                if p >= 2:
                    w.append(tk_pv2[p - 2])
                tk_pv1a[p] = P.step("vector", _pv1a, waits=w)

                def _pv1(eng, p=p, A0=A0, A1=A1):
                    for A in (A0, A1):
                        eng.tensor_copy(A["n1f"][:, 0:ch], A["tmp"][:, 0:ch])
                    eng.drain()
                    for A in (A0, A1):
                        eng.tensor_scalar(A["tmp"][:, 0:ch], A["n1f"][:, 0:ch], 1.0 / 64.0,
                                          None, ALU.mult)
                    eng.drain()
                    for A in (A0, A1):
                        eng.scalar_tensor_tensor(A["e"][:, 0:ch], A["tmp"][:, 0:ch], -64.0,
                                                 A["n1f"][:, 0:ch], ALU.mult, ALU.add)
                    eng.drain()
                    for A in (A0, A1):
                        eng.tensor_scalar(A["f"][:, 0:ch], A["e"][:, 0:ch], 0.0, 64.0,
                                          ALU.is_lt, ALU.mult)
                    eng.drain()
                    for A in (A0, A1):
                        eng.tensor_tensor(A["tmp"][:, 0:ch], A["e"][:, 0:ch], A["f"][:, 0:ch],
                                          ALU.add)
                    eng.drain()
                    for A in (A0, A1):
                        eng.scalar_tensor_tensor(A["e"][:, 0:ch], A["tmp"][:, 0:ch], -1.0,
                                                 A["n1f"][:, 0:ch], ALU.mult, ALU.add)
                tk_pv1[p] = P.step("vector", _pv1)

                def _pgs(eng, p=p, A0=A0, A1=A1):
                    for A in (A0, A1):
                        eng.activation(A["f"][:, 0:ch], A["e"][:, 0:ch], ACT.Gelu,
                                       scale=1.0 / 256.0, bias=t_cst[:, C_GB0:C_GB0 + 1])
                        eng.activation(A["gg"][:, 0:ch], A["e"][:, 0:ch], ACT.Gelu,
                                       scale=1.0 / 256.0, bias=t_cst[:, C_GB1:C_GB1 + 1])
                tk_pgs[p] = P.step("scalar", _pgs, waits=[tk_pv1[p]])

                if p >= 1:
                    emit_pv2(p - 1)
            emit_pv2(NP - 1)
            prev_ffB = tk_pv2[NP - 1]
            if "ffg" in dbg:
                P.dma("gpsimd", "dbg",
                      [(dbg["ffg"][mf * 128:(mf + 1) * 128, j * ch:(j + 1) * ch],
                        t_gfp[mf][:, 0:ch]) for mf in range(FG)], waits=[prev_ffB])

            tk_out = {}
            for md in range(DG):
                base = w2_cnt[0]
                for pq in range(FG // 4):
                    pk = (base + pq * 4) // 4
                    ii = md * FG + pq * 4
                    P.dma("sync", "w2_%d" % (pk % 4),
                          [(t_wb2[pk % 4][:].rearrange("p (b c) -> p b c", b=4),
                            i_w2[ii * 128:(ii + 4) * 128, :]
                            .rearrange("(b p) c -> p b c", b=4))],
                          waits=[tk_small] if pk < 4 else [Tok(P.csem("w2"), 4 * (pk - 4) + 4)])
                w2_cnt[0] += FG

                def _mm2(eng, bump, md=md, j=j, base=base):
                    for k in range(FG):
                        i = base + k
                        pk = i // 4
                        eng.wait_ge(P.dma_sem("w2_%d" % (pk % 4)), 16 * (pk // 4 + 1))
                        sl = t_wb2[pk % 4][:, 128 * (i % 4):128 * (i % 4) + 128]
                        r = eng.matmul(pB[md % 2][0:128, 0:ch], sl,
                                       t_gfp[k][:, 0:ch], start=(k == 0), stop=(k == FG - 1))
                        if j == 0:
                            bank = pSS if md % 2 == 0 else pRp
                            r = eng.matmul(bank[0:128, 0:1], sl, t_onesc[:],
                                           start=(k == 0), stop=(k == FG - 1))
                        r.then_inc(P.csem("w2"), 1)
                mm2_waits = [prev_ffB]
                if md >= 2:
                    mm2_waits.append(tk_out[md - 2])
                if j == 0 and md >= 2:
                    mm2_waits.append(tk_b2[md - 2])
                if j == 0 and md < 2:
                    mm2_waits.append(tk_h2)
                P.raw("tensor", _mm2, waits=mm2_waits)
                tk_mm2 = Tok(P.csem("w2"), base + FG)

                if j == 0:
                    def _b2c(eng, md=md):
                        bank = pSS if md % 2 == 0 else pRp
                        eng.scalar_tensor_tensor(t_bias2[:, md:md + 1], bank[0:128, 0:1],
                                                 -OFF15, t_b2p[:, md:md + 1],
                                                 ALU.mult, ALU.add)
                    tk_b2[md] = P.step("vector", _b2c, waits=[tk_mm2])

                def _outf(eng, md=md, j=j):
                    eng.tensor_scalar(t_n1[:, 0:ch], pB[md % 2][0:128, 0:ch], 1.0,
                                      t_bias2[:, md:md + 1], ALU.mult, ALU.add)
                    eng.drain()
                    eng.scalar_tensor_tensor(t_e[:, 0:ch], t_n1[:, 0:ch],
                                             t_gb256[:, md:md + 1], t_x1[md][:, 0:ch],
                                             ALU.mult, ALU.add)
                    eng.drain()
                    eng.tensor_scalar(t_x[md][:, 0:ch], t_e[:, 0:ch],
                                      t_gbfix[:, md:md + 1], None, ALU.add)
                ow = [tk_mm2]
                ow.append(tk_b2[md] if j == 0 else tk_bias2_done)
                tk_out[md] = P.step("vector", _outf, waits=ow)
            if j == 0:
                tk_bias2_done = tk_b2[DG - 1]
            prev_mm2 = tk_out[DG - 1]
            tk_store[j] = P.dma("sync", "store",
                                [(o_y[g * 128:(g + 1) * 128, j * ch:(j + 1) * ch],
                                  t_x[g][:, 0:ch]) for g in range(DG)],
                                waits=[tk_out[DG - 1]])

        def _bar(eng):
            for s in ("store", "dbg"):
                if s in P.dsems:
                    eng.wait_ge(P.dma_sem(s), P.dvals[s])
        P.step("sync", _bar)

    return nc



def build_fused(T_core, ch, debug_fe=(), debug_be=()):
    nc = bass.Bass("TRN2", target_bir_lowering=False)
    es = contextlib.ExitStack()
    with es:
        F = {"nc": nc}
        F["i_xT"] = nc.dram_tensor("xT", [D, T_core + 3], F32, kind="ExternalInput").ap()
        F["sp_cs"] = nc.dram_tensor("sp_cs", [NS * 128, T_core], F32).ap()
        F["sp_gate"] = nc.dram_tensor("sp_gate", [8 * 128, T_core], F32).ap()
        F["sp_den"] = nc.dram_tensor("sp_den", [32, T_core], F32).ap()
        F["cc_in"] = nc.dram_tensor("cc_in", [NCORE * 128, 66], F32)
        F["cc_out"] = nc.dram_tensor("cc_out", [NCORE * 128, 66], F32)
        F["cc_out"] = F["cc_out"]
        F["t_mod"] = es.enter_context(nc.sbuf_tensor("s_mod", [128, 48], F32))
        F["t_r32"] = es.enter_context(nc.sbuf_tensor("s_r32", [32, 8 * 128], F32))
        F["t_ones1"] = es.enter_context(nc.sbuf_tensor("s_ones1", [1, 128], F32))
        F["t_onesc"] = es.enter_context(nc.sbuf_tensor("s_onesc", [128, 1], F32))
        F["t_cst"] = es.enter_context(nc.sbuf_tensor("s_cst", [128, NCST], F32))
        F["block"] = es.enter_context(nc.Block())
        F["P"] = Prog(nc, F["block"], es)
        P = F["P"]

        build_fe(T_core, ch, debug=debug_fe, F=F)

        t_junk = es.enter_context(nc.sbuf_tensor("t_junk", [128, 1024], F32))

        def _cc(eng):
            eng.wait_ge(P.dma_sem("ccin"), P.dvals["ccin"])
            for i in range(NCORE):
                eng.dma_start(out=t_junk[:, 66 * i:66 * (i + 1)],
                              in_=F["cc_in"].ap()[i * 128:(i + 1) * 128, 0:66]
                              ).then_inc(P.csem("ccrb"), 16)
            eng.wait_ge(P.csem("ccrb"), 16 * NCORE)
            for i in range(NCORE):
                eng.dma_start(out=t_junk[:, 66 * i:66 * (i + 1)],
                              in_=F["cc_in"].ap()[i * 128:(i + 1) * 128, 0:66]
                              ).then_inc(P.csem("ccrb"), 16)
            eng.wait_ge(P.csem("ccrb"), 32 * NCORE)
            if CC_DELAY_PRE:
                for _ in range(CC_DELAY_PRE):
                    eng.memset(t_junk[:], 0.0)
                    eng.drain()
            r = eng.collective_compute(
                "AllReduce", ALU.add, replica_groups=[list(range(NCORE))],
                ins=[F["cc_in"].ap().opt()], outs=[F["cc_out"].ap().opt()])
            if CC_DELAY_POST:
                r.then_inc(P.csem("ccflush"), 1)
                eng.wait_ge(P.csem("ccflush"), 1)
                for _ in range(CC_DELAY_POST):
                    eng.memset(t_junk[:], 0.0)
                    eng.drain()
                return None
            return r
        F["cc_done"] = P.step("gpsimd", _cc, waits=F["fe_bar"], attach=True)
        if DEBUG_CC:
            dci = nc.dram_tensor("dbg_ccin", [NCORE * 128, 66], F32, kind="ExternalOutput").ap()
            dco = nc.dram_tensor("dbg_ccout", [NCORE * 128, 66], F32, kind="ExternalOutput").ap()
            P.dma("gpsimd", "dbg", [(dci, F["cc_in"].ap()), (dco, F["cc_out"].ap())],
                  waits=[F["cc_done"]])

        build_be(T_core, ch, debug=debug_be, F=F)
    return nc



def _tile_rows(w, nblk_r, nblk_c, bs_r=128, bs_c=128):
    R, C = w.shape
    out = np.zeros((nblk_r * nblk_c * bs_r, bs_c), dtype=w.dtype)
    for i in range(nblk_r):
        for jj in range(nblk_c):
            blk = w[i * bs_r:(i + 1) * bs_r, jj * bs_c:(jj + 1) * bs_c]
            r0 = (i * nblk_c + jj) * bs_r
            out[r0:r0 + blk.shape[0], 0:blk.shape[1]] = blk
    return out


def _fcol(v, parts=128):
    n = v.shape[0]
    k = n // parts
    return np.ascontiguousarray(v.reshape(k, parts).T)


_FE_CACHE = {}
_BE_CACHE = {}


def kernel(L0_attn_x, L0_attn_W_in, L0_attn_w_conv, L0_attn_b_conv, L0_attn_theta,
           L0_attn_decay_slopes, L0_attn_anchor_slopes, L0_attn_score_scale,
           L0_attn_deriv_logits, L0_attn_norm_scale, L0_attn_W_re, L0_attn_W_im,
           L0_attn_W_out, L0_ffn_w1, L0_ffn_b1, L0_ffn_w2, L0_ffn_b2, L0_c, L0_wmod,
           _debug_fe=(), _debug_be=(), _t_core=T_CORE_DEFAULT, _ch=CH_DEFAULT,
           _fe_only=False, _fused=True):
    T_core = _t_core
    ch = _ch
    x = np.ascontiguousarray(np.asarray(L0_attn_x, np.float32))
    B, L, _ = x.shape
    nq = NCORE // B
    assert T_core * nq == L or T_core != T_CORE_DEFAULT

    f32 = lambda a: np.ascontiguousarray(np.asarray(a, np.float32))
    W_in = f32(L0_attn_W_in)
    w_conv = f32(L0_attn_w_conv)
    b_conv = f32(L0_attn_b_conv)
    theta = f32(L0_attn_theta)
    dsl = f32(L0_attn_decay_slopes)
    asl = f32(L0_attn_anchor_slopes)
    score = f32(L0_attn_score_scale)
    dlog = f32(L0_attn_deriv_logits)
    nsc = f32(L0_attn_norm_scale)
    W_re = f32(L0_attn_W_re)
    W_im = f32(L0_attn_W_im)
    W_out = f32(L0_attn_W_out)
    w1 = f32(L0_ffn_w1)
    b1 = f32(L0_ffn_b1)
    w2 = f32(L0_ffn_w2)
    b2 = f32(L0_ffn_b2)
    cmat = f32(L0_c)
    wmod = f32(L0_wmod)

    win_t = _tile_rows(W_in, ZG, 8, 128, 128)
    win_t = np.zeros((ZG * 8 * 128, 128), np.float32)
    for mz in range(ZG):
        for k in range(8):
            blk = W_in[k * 128:(k + 1) * 128, mz * 128:mz * 128 + _zsz(mz)]
            r0 = (mz * 8 + k) * 128
            win_t[r0:r0 + 128, 0:blk.shape[1]] = blk
    wmod_t = np.zeros((12 * 8 * 128, 512), np.float32)
    for cc in range(12):
        for k in range(8):
            wmod_t[(cc * 8 + k) * 128:(cc * 8 + k + 1) * 128, :] = \
                wmod[k * 128:(k + 1) * 128, cc * 512:(cc + 1) * 512]
    wcv = np.zeros((128, 4 * ZG), np.float32)
    bcv = np.zeros((128, ZG), np.float32)
    for mz in range(ZG):
        zs = _zsz(mz)
        for jj in range(4):
            wcv[0:zs, jj * ZG + mz] = w_conv[jj, 0, mz * 128:mz * 128 + zs]
        bcv[0:zs, mz] = b_conv[mz * 128:mz * 128 + zs]
    th_flat = theta.reshape(D, MTH)
    thetas = np.zeros((128, 32), np.float32)
    for g in range(8):
        for m in range(MTH):
            thetas[:, g * 4 + m] = th_flat[g * 128:(g + 1) * 128, m]
    slvec = np.zeros((32, 2), np.float32)
    slvec[:, 0] = np.concatenate([dsl, asl])
    slvec[:28, 1] = 1.0
    slvec[28:, 1] = -1.0
    scorev = score.reshape(32, 1)
    dlog2 = dlog.reshape(1, 3)
    hm1 = np.ones((128, 3), np.float32)
    hm0 = np.zeros((128, 3), np.float32)
    r32 = np.zeros((32, 8 * 128), np.float32)
    for g in range(8):
        for c_ in range(128):
            r32[4 * g + c_ // 32, g * 128 + c_] = 1.0
    ones1 = np.ones((1, 128), np.float32)
    onesc = np.ones((128, 1), np.float32)
    cstrow = np.zeros(NCST, np.float32)
    cstrow[C_EPSLN] = np.float32(1e-5)
    cstrow[C_EPSRMS] = np.float32(1e-5)
    cstrow[C_ONE] = 1.0
    cstrow[C_GB0] = -128.0
    cstrow[C_GB1] = -127.75
    cst = np.tile(cstrow[None, :], (128, 1)).astype(np.float32)

    w1_t = np.zeros((FG * 8 * 128, 128), np.float32)
    for mf in range(FG):
        for k in range(8):
            w1_t[(mf * 8 + k) * 128:(mf * 8 + k + 1) * 128, :] = \
                w1[k * 128:(k + 1) * 128, mf * 128:(mf + 1) * 128]
    w2_t = np.zeros((8 * FG * 128, 128), np.float32)
    for md in range(8):
        for k in range(FG):
            w2_t[(md * FG + k) * 128:(md * FG + k + 1) * 128, :] = \
                w2[k * 128:(k + 1) * 128, md * 128:(md + 1) * 128]
    wo_t = np.zeros((8 * 8 * 128, 128), np.float32)
    for md in range(8):
        for k in range(8):
            wo_t[(md * 8 + k) * 128:(md * 8 + k + 1) * 128, :] = \
                W_out[k * 128:(k + 1) * 128, md * 128:(md + 1) * 128]
    wrebd = np.zeros((8 * 128, 128), np.float32)
    nsccols = np.zeros((128, 8), np.float32)
    for m in range(MTH):
        for side in range(2):
            i = 2 * m + side
            Wsrc = W_re if side == 0 else W_im
            Wm = Wsrc[m::MTH, :]
            blk = np.zeros((128, 128), np.float32)
            for kk in range(4):
                blk[kk * 32:(kk + 1) * 32, kk * 32:(kk + 1) * 32] = Wm
            wrebd[i * 128:(i + 1) * 128, :] = blk
            for p in range(128):
                nsccols[p, i] = nsc[side * 128 + (p % 32) * MTH + m]
    obd32 = np.zeros((128, 8 * 32), np.float32)
    for g in range(8):
        for p_ in range(128):
            obd32[p_, g * 32 + 4 * g + p_ // 32] = 1.0
    b1c = _fcol(b1)
    b2c = _fcol(b2)

    if _fused:
        key = ("fused", T_core, ch, tuple(_debug_fe), tuple(_debug_be))
        if key not in _FE_CACHE:
            _FE_CACHE[key] = build_fused(T_core, ch, _debug_fe, _debug_be)
        nc_f = _FE_CACHE[key]
    else:
        key = (T_core, ch, tuple(_debug_fe))
        if key not in _FE_CACHE:
            _FE_CACHE[key] = build_fe(T_core, ch, _debug_fe)
        nc_fe = _FE_CACHE[key]

    in_maps = []
    for core in range(NCORE):
        b, q = core // nq, core % nq
        t0 = q * T_core
        if t0 == 0:
            xs = np.concatenate([np.zeros((3, D), np.float32), x[b, 0:T_core + 0]], axis=0)
            hm = hm0
        else:
            xs = x[b, t0 - 3:t0 + T_core]
            hm = hm1
        xT = np.ascontiguousarray(xs.T)
        tb = np.zeros((32, 2), np.float32)
        tb[:28, 0] = -float(L - 1 - t0)
        tb[28:, 0] = float(t0)
        in_maps.append({
            "xT": xT, "win": win_t, "wmod": wmod_t, "wcv": wcv, "bcv": bcv,
            "thetas": thetas, "slvec": slvec, "scorev": scorev, "dlog": dlog2,
            "crow": np.ascontiguousarray(cmat[b].reshape(8, 128).T),
            "hmask": hm, "r32": r32, "ones_k1": ones1, "ones_col": onesc,
            "tbase": tb, "cst": cst,
        })
        if _fused:
            wsl = np.zeros((128, NCORE), np.float32)
            wsl[:, core] = 1.0
            smask = np.zeros((128, NCORE), np.float32)
            for i in range(NCORE):
                ib_, iq_ = i // nq, i % nq
                if ib_ == b and iq_ < q:
                    smask[:, i] = 1.0
            in_maps[-1].update({
                "smask": smask, "wsl": wsl,
                "w1t": w1_t, "w2t": w2_t, "wot": wo_t, "wrebd": wrebd,
                "nsccols": nsccols, "obd32": obd32, "b1c": b1c, "b2c": b2c,
            })

    if _fused:
        res = run_bass_kernel_spmd(nc_f, in_maps, core_ids=list(range(NCORE)))
        be = res.results
        if _debug_fe or _debug_be:
            return be
        out = np.empty((B, L, D), np.float32)
        for core in range(NCORE):
            b, q = core // nq, core % nq
            out[b, q * T_core:(q + 1) * T_core, :] = be[core]["yT"].T
        return out

    res_fe = run_bass_kernel_spmd(nc_fe, in_maps, core_ids=list(range(NCORE)))
    fe = res_fe.results
    if _fe_only:
        return fe

    s_all = np.concatenate([fe[i]["s_out"] for i in range(NCORE)], axis=0)

    key = (T_core, ch, tuple(_debug_be))
    if key not in _BE_CACHE:
        _BE_CACHE[key] = build_be(T_core, ch, _debug_be)
    nc_be = _BE_CACHE[key]

    in_maps2 = []
    for core in range(NCORE):
        b, q = core // nq, core % nq
        t0 = q * T_core
        smask = np.zeros((128, NCORE), np.float32)
        for i in range(NCORE):
            ib, iq = i // nq, i % nq
            if ib == b and iq < q:
                smask[:, i] = 1.0
        in_maps2.append({
            "xT": in_maps[core]["xT"],
            "sp_cs": fe[core]["sp_cs"], "sp_gate": fe[core]["sp_gate"],
            "sp_den": fe[core]["sp_den"], "s_all": s_all, "smask": smask,
            "mod_row": fe[core]["mod_out"],
            "w1t": w1_t, "w2t": w2_t, "wot": wo_t, "wrebd": wrebd,
            "nsccols": nsccols, "obd32": obd32, "b1c": b1c, "b2c": b2c,
            "r32": r32, "ones_k1": ones1, "ones_col": onesc, "cst": cst,
        })

    res_be = run_bass_kernel_spmd(nc_be, in_maps2, core_ids=list(range(NCORE)))
    be = res_be.results
    if _debug_be:
        return fe, be

    out = np.empty((B, L, D), np.float32)
    for core in range(NCORE):
        b, q = core // nq, core % nq
        out[b, q * T_core:(q + 1) * T_core, :] = be[core]["yT"].T
    return out


# version 16
# speedup vs baseline: 1.0255x; 1.0255x over previous
import contextlib
import numpy as np

import concourse.bass as bass
import concourse.mybir as mybir
from concourse.bass_utils import run_bass_kernel_spmd

F32 = mybir.dt.float32
I32 = mybir.dt.int32
ALU = mybir.AluOpType
ACT = mybir.ActivationFunctionType
AXL = mybir.AxisListType

D = 1024
KH = 32
HD = 32
MTH = 4
TOT = 2 * D + KH
FF = 4096
LSEQ = 8192
NCORE = 8
DG = D // 128
ZG = 17
ZLAST = TOT - 16 * 128
FG = FF // 128
NS = 64
OFF15 = 32768.0

C_EPSLN = 0
C_EPSRMS = 1
C_ONE = 2
C_GB0 = 3
C_GB1 = 4
NCST = 6

T_CORE_DEFAULT = 2048
CH_DEFAULT = 512
DEBUG_CC = False
CC_DELAY_PRE = 25
CC_DELAY_POST = 25


def _zsz(mz):
    return 128 if mz < 16 else ZLAST


def _scol(g, m, side):
    return side * 32 + g * 4 + m


class Tok:
    __slots__ = ("sem", "val")

    def __init__(self, sem, val):
        self.sem = sem
        self.val = val


class Prog:
    ENGINES = ("tensor", "vector", "scalar", "gpsimd", "sync")

    def __init__(self, nc, block, es):
        self.nc = nc
        self.block = block
        self.sems = {}
        self.vals = {}
        for e in self.ENGINES:
            self.sems[e] = es.enter_context(nc.semaphore("ms_" + e))
            self.vals[e] = 0
        self.dsems = {}
        self.dvals = {}
        self.es = es

    def dma_sem(self, name):
        if name not in self.dsems:
            self.dsems[name] = self.es.enter_context(self.nc.semaphore("dm_" + name))
            self.dvals[name] = 0
        return self.dsems[name]

    def step(self, engine, fn, waits=(), attach=False):
        dec = getattr(self.block, engine)
        sem = self.sems[engine]

        @dec
        def _(eng, fn=fn, waits=tuple(w for w in waits if w is not None)):
            for t in waits:
                eng.wait_ge(t.sem, t.val)
            r = fn(eng)
            if attach and r is not None:
                r.then_inc(sem, 1)
            else:
                eng.drain().then_inc(sem, 1)

        self.vals[engine] += 1
        return Tok(sem, self.vals[engine])

    def raw(self, engine, fn, waits=()):
        dec = getattr(self.block, engine)
        vals = self.vals

        def bump():
            vals[engine] += 1
            return vals[engine]

        @dec
        def _(eng, fn=fn, waits=tuple(w for w in waits if w is not None)):
            for t in waits:
                eng.wait_ge(t.sem, t.val)
            fn(eng, bump)

        return Tok(self.sems[engine], self.vals[engine])

    def csem(self, name):
        return self.dma_sem("c_" + name)

    def dma(self, queue_engine, stream, transfers, waits=()):
        sem = self.dma_sem(stream)
        dec = getattr(self.block, queue_engine)

        @dec
        def _(eng, transfers=tuple(transfers), waits=tuple(w for w in waits if w is not None)):
            for t in waits:
                eng.wait_ge(t.sem, t.val)
            for tr in transfers:
                kw = tr[2] if len(tr) > 2 else {}
                eng.dma_start(out=tr[0], in_=tr[1], **kw).then_inc(sem, 16)

        self.dvals[stream] += 16 * len(transfers)
        return Tok(sem, self.dvals[stream])



def build_fe(T_core, ch, debug=(), F=None):
    nch = T_core // ch
    if F is None:
        nc = bass.Bass("TRN2", target_bir_lowering=False)
        i_xT = nc.dram_tensor("xT", [D, T_core + 3], F32, kind="ExternalInput").ap()
    else:
        nc = F["nc"]
        i_xT = F["i_xT"]

    i_win = nc.dram_tensor("win", [ZG * 8 * 128, 128], F32, kind="ExternalInput").ap()
    i_wmod = nc.dram_tensor("wmod", [12 * 8 * 128, 512], F32, kind="ExternalInput").ap()
    i_wcv = nc.dram_tensor("wcv", [128, 4 * ZG], F32, kind="ExternalInput").ap()
    i_bcv = nc.dram_tensor("bcv", [128, ZG], F32, kind="ExternalInput").ap()
    i_theta = nc.dram_tensor("thetas", [128, 32], F32, kind="ExternalInput").ap()
    i_sl = nc.dram_tensor("slvec", [32, 2], F32, kind="ExternalInput").ap()
    i_score = nc.dram_tensor("scorev", [32, 1], F32, kind="ExternalInput").ap()
    i_dlog = nc.dram_tensor("dlog", [1, 3], F32, kind="ExternalInput").ap()
    i_crow = nc.dram_tensor("crow", [128, 8], F32, kind="ExternalInput").ap()
    i_hmask = nc.dram_tensor("hmask", [128, 3], F32, kind="ExternalInput").ap()
    i_r32 = nc.dram_tensor("r32", [32, 8 * 128], F32, kind="ExternalInput").ap()
    i_ones1 = nc.dram_tensor("ones_k1", [1, 128], F32, kind="ExternalInput").ap()
    i_onesc = nc.dram_tensor("ones_col", [128, 1], F32, kind="ExternalInput").ap()
    i_tb = nc.dram_tensor("tbase", [32, 2], F32, kind="ExternalInput").ap()
    i_cst = nc.dram_tensor("cst", [128, NCST], F32, kind="ExternalInput").ap()
    if F is not None:
        i_wsl = nc.dram_tensor("wsl", [128, NCORE], F32, kind="ExternalInput").ap()

    if F is None:
        o_s = nc.dram_tensor("s_out", [128, 66], F32, kind="ExternalOutput").ap()
        o_cs = nc.dram_tensor("sp_cs", [NS * 128, T_core], F32, kind="ExternalOutput").ap()
        o_gate = nc.dram_tensor("sp_gate", [8 * 128, T_core], F32, kind="ExternalOutput").ap()
        o_den = nc.dram_tensor("sp_den", [32, T_core], F32, kind="ExternalOutput").ap()
        o_mod = nc.dram_tensor("mod_out", [1, 6 * D], F32, kind="ExternalOutput").ap()
    else:
        o_cs = F["sp_cs"]
        o_gate = F["sp_gate"]
        o_den = F["sp_den"]
    dbg_shapes = {"h": [DG * 128, T_core], "cz": [ZG * 128, T_core],
                  "pw": [32, T_core], "mg": [NS * 128, T_core]}
    dbg = {nm: nc.dram_tensor("dbg_" + nm, dbg_shapes[nm], F32, kind="ExternalOutput").ap()
           for nm in debug}

    es = contextlib.ExitStack()
    with es:
        def sb(name, shape, dt=F32):
            return es.enter_context(nc.sbuf_tensor(name, shape, dt))

        def pt(name):
            return es.enter_context(nc.psum_tensor(name, [128, 512], F32))

        t_wcv = sb("t_wcv", [128, 4 * ZG])
        _ = None
        t_bcv = sb("t_bcv", [128, ZG])
        t_theta = sb("t_theta", [128, 32])
        t_th4 = sb("t_th4", [128, 32])
        t_th8 = sb("t_th8", [128, 32])
        t_invP = sb("t_invP", [128, 32])
        t_negP = sb("t_negP", [128, 32])
        t_sl = sb("t_sl", [32, 2])
        t_sp = sb("t_sp", [32, 1])
        t_nsp = sb("t_nsp", [32, 1])
        t_score = sb("t_score", [32, 1])
        t_dlog = sb("t_dlog", [1, 3])
        t_wpoly = sb("t_wpoly", [128, 4])
        t_crow = sb("t_crow", [128, 8])
        t_silu = sb("t_silu", [128, 8])
        if F is None:
            t_mod = sb("t_mod", [128, 48])
            t_r32 = sb("t_r32", [32, 8 * 128])
            t_ones1 = sb("t_ones1", [1, 128])
            t_onesc = sb("t_onesc", [128, 1])
            t_cst = sb("t_cst", [128, NCST])
        else:
            t_mod = F["t_mod"]
            t_r32 = F["t_r32"]
            t_ones1 = F["t_ones1"]
            t_onesc = F["t_onesc"]
            t_cst = F["t_cst"]
        t_ca1 = sb("t_ca1", [128, 8])
        t_hmask = sb("t_hmask", [128, 3])
        t_tb = sb("t_tb", [32, 2])
        t_iota = sb("t_iota", [32, ch])
        t_row = sb("t_row", [1, 16])
        t_carry = sb("t_carry", [128, NS])
        t_dcar = sb("t_dcar", [32, 1])
        t_stat = sb("t_stat", [1, 4 * ch])
        t_hst = sb("t_hst", [1, 12])
        t_db = sb("t_db", [32, 1])
        t_db2 = sb("t_db2", [32, 1])
        t_sfin = sb("t_sfin", [128, 66])
        if F is not None:
            t_wsl = sb("t_wsl", [128, NCORE])
            t_sall8 = sb("t_sall8", [128, 66 * NCORE])

        t_x = [sb(f"t_x{g}", [128, ch + 3]) for g in range(DG)]
        t_h = [sb(f"t_h{g}", [128, ch + 3]) for g in range(DG)]
        t_sq = [sb(f"t_sq{g}", [128, ch + 3]) for g in range(DG)]
        t_z = [sb(f"t_z{mz}", [128, ch + 3]) for mz in range(ZG)]
        t_zst = sb("t_zst", [128, 3 * ZG])
        t_cz2 = [sb(f"t_cz{mz}", [128, ch]) for mz in range(8, ZG)]
        t_cz = [t_h[g] for g in range(8)] + t_cz2
        t_gate = t_sq
        t_mg = [sb(f"t_mg{i}", [128, ch]) for i in range(8)]
        t_cs = [sb(f"t_cs{i}", [128, ch]) for i in range(8)]
        if ch >= 512:
            t_mrow = [t_cs[p] for p in range(2)]
        else:
            t_mrow = [sb(f"t_mrow{p}", [1, 512]) for p in range(2)]
        t_dencs = sb("t_dencs", [32, ch])
        t_tw = sb("t_tw", [32, ch])
        t_pw = sb("t_pw", [32, ch])
        t_pwa = sb("t_pwa", [32, ch])
        t_pwb = sb("t_pwb", [32, ch])
        t_pp = [sb(f"t_pp{p}", [128, ch]) for p in range(2)]
        t_ta = [sb(f"t_ta{m}", [128, ch]) for m in range(MTH)]
        t_ki = sb("t_ki", [128, ch], I32)
        t_ki2 = sb("t_ki2", [128, ch], I32)
        t_pih = sb("t_pih", [128, 1])
        t_s8 = [sb(f"t_s8{m}", [128, ch]) for m in range(MTH)]
        t_s4 = [sb(f"t_s4{m}", [128, ch]) for m in range(MTH)]
        t_e = sb("t_e", [128, ch])
        t_f = sb("t_f", [128, ch])
        t_win = [sb(f"t_win{i}", [128, 512]) for i in range(4)]
        if ch >= 512:
            t_wmodb = [t_mg[i][:, 0:512].tensor for i in range(4)]
            t_wmodb = [t_mg[i] for i in range(4)]
        else:
            t_wmodb = [sb(f"t_wmodb{i}", [128, 512]) for i in range(4)]

        if F is not None:
            F["loot"] = (t_mg, t_cs, t_ki, t_ki2)

        pZ = [pt("pZ0"), pt("pZ1")]
        pS1 = pt("pS1")
        pS2 = pt("pS2")
        pR = [pt("pR0"), pt("pR1")]
        pP = [pt("pP0"), pt("pP1")]

        if F is None:
            block = es.enter_context(nc.Block())
            P = Prog(nc, block, es)
        else:
            block = F["block"]
            P = F["P"]

        tk_small = P.dma("sync", "fesmall", [
            (t_wcv[:], i_wcv), (t_bcv[:], i_bcv), (t_theta[:], i_theta),
            (t_sl[:], i_sl), (t_score[:], i_score), (t_dlog[:], i_dlog),
            (t_crow[:], i_crow), (t_hmask[:], i_hmask), (t_r32[:], i_r32),
            (t_ones1[:], i_ones1), (t_onesc[:], i_onesc), (t_tb[:], i_tb),
            (t_cst[:], i_cst),
        ] + ([(t_wsl[:], i_wsl)] if F is not None else []))

        def _su_g(eng):
            eng.iota(t_iota[:], [[1, ch]], base=0, channel_multiplier=0, allow_small_or_imprecise_dtypes=True)
            eng.memzero(t_carry[:])
            eng.memzero(t_dcar[:])
        tk_iota = P.step("gpsimd", _su_g)

        def _su_s1(eng):
            eng.activation(t_silu[:], t_crow[:], ACT.Silu)
            eng.activation(t_nsp[:], t_sl[:, 0:1], ACT.Exp)
            eng.drain()
            eng.activation(t_sp[:], t_nsp[:], ACT.Ln, bias=t_cst[0:32, C_ONE:C_ONE + 1])
            eng.activation(t_row[0:1, 0:1], t_ones1[0:1, 0:1], ACT.Arctan)
        tk_s1 = P.step("scalar", _su_s1, waits=[tk_small])

        def _su_v1(eng):
            eng.tensor_tensor(t_nsp[:], t_sp[:], t_sl[:, 1:2], ALU.mult)
            eng.tensor_scalar(t_row[0:1, 1:2], t_row[0:1, 0:1], 8.0, None, ALU.mult)
            eng.tensor_scalar(t_row[0:1, 13:14], t_row[0:1, 0:1], 2.0, None, ALU.mult)
            eng.tensor_reduce(t_row[0:1, 2:3], t_dlog[:], AXL.X, ALU.max)
            eng.drain()
            eng.tensor_scalar(t_row[0:1, 3:4], t_row[0:1, 2:3], -1.0, None, ALU.mult)
            eng.reciprocal(t_row[0:1, 4:5], t_row[0:1, 1:2])
        tk_v1 = P.step("vector", _su_v1, waits=[tk_s1])

        def _su_s2(eng):
            eng.activation(t_row[0:1, 5:8], t_dlog[:], ACT.Exp, bias=t_row[0:1, 3:4])
        tk_s2 = P.step("scalar", _su_s2, waits=[tk_v1])

        def _su_v2(eng):
            eng.tensor_reduce(t_row[0:1, 8:9], t_row[0:1, 5:8], AXL.X, ALU.add)
            eng.drain()
            eng.reciprocal(t_row[0:1, 9:10], t_row[0:1, 8:9])
            eng.drain()
            eng.tensor_scalar(t_row[0:1, 10:13], t_row[0:1, 5:8], t_row[0:1, 9:10], None, ALU.mult)
        tk_v2 = P.step("vector", _su_v2, waits=[tk_s2])

        def _su_t1(eng):
            eng.matmul(pP[0][0:128, 0:3], t_ones1[:], t_row[0:1, 10:13], start=True, stop=False)
            eng.matmul(pP[0][0:128, 3:4], t_ones1[:], t_row[0:1, 4:5],
                       start=False, stop=False)
            return eng.matmul(pP[0][0:128, 4:5], t_ones1[:], t_row[0:1, 13:14],
                              start=False, stop=True)
        tk_t1 = P.step("tensor", _su_t1, waits=[tk_v2], attach=True)

        def _su_v3(eng):
            eng.tensor_copy(t_wpoly[:], pP[0][0:128, 0:4])
            eng.tensor_copy(t_pih[:], pP[0][0:128, 4:5])
            eng.drain()
            eng.tensor_scalar(t_invP[:], t_theta[:], t_wpoly[:, 3:4], None, ALU.mult)
            eng.tensor_scalar(t_th4[:], t_theta[:], 0.25, None, ALU.mult)
            eng.tensor_scalar(t_th8[:], t_theta[:], 0.125, None, ALU.mult)
            eng.drain()
            eng.reciprocal(t_negP[:], t_invP[:])
            eng.drain()
            eng.tensor_scalar(t_negP[:], t_negP[:], -1.0, None, ALU.mult)
        tk_v3 = P.step("vector", _su_v3, waits=[tk_t1])

        tk_cc_ev = {}
        tk_mod_mm = {}
        tk_mc = {}
        for i in range(96):
            cc, k = divmod(i, 8)
            dma_w = [tk_small] if i < 4 else [tk_mod_mm[i - 4]]
            P.dma("sync", "wm%d" % (i % 4),
                  [(t_wmodb[i % 4][0:128, 0:512], i_wmod[i * 128:(i + 1) * 128, :])],
                  waits=dma_w)

            def _mm(eng, cc=cc, k=k, i=i):
                eng.wait_ge(P.dma_sem("wm%d" % (i % 4)), 16 * (i // 4 + 1))
                dst = pP[1] if cc % 2 else pP[0]
                return eng.matmul(dst[0:1, 0:512], t_silu[:, k:k + 1],
                                  t_wmodb[i % 4][0:128, 0:512],
                                  start=(k == 0), stop=(k == 7))
            w = [tk_s1, tk_v3] if i == 0 else ()
            if k == 0 and cc >= 2:
                w = [tk_cc_ev[cc - 2]]
            t = P.step("tensor", _mm, waits=w, attach=True)
            tk_mod_mm[i] = t
            if k == 7:
                def _ev(eng, cc=cc):
                    src = pP[1] if cc % 2 else pP[0]
                    eng.tensor_copy(t_mrow[cc % 2][0:1, 0:512], src[0:1, 0:512])
                tk_cc_ev[cc] = P.step("vector", _ev,
                                      waits=[t, tk_mc.get(cc - 2)])
                trs = ([] if F is not None else
                       [(o_mod[0:1, cc * 512:(cc + 1) * 512], t_mrow[cc % 2][0:1, 0:512])])
                for s_ in range(4):
                    f0 = 512 * cc + 128 * s_
                    col = (f0 // D) * 8 + (f0 % D) // 128
                    trs.append((t_mod[:, col:col + 1], t_mrow[cc % 2][0:1, 128 * s_:128 * (s_ + 1)],
                                dict(allow_slow_non_contiguous=True)))
                tk_mc[cc] = P.dma("sync", "mc%d" % (cc % 2), trs, waits=[tk_cc_ev[cc]])

        def _su_v4(eng):
            eng.tensor_scalar(t_ca1[:], t_mod[:, 8:16], 1.0, None, ALU.add)
        tk_v4 = P.step("vector", _su_v4, waits=[tk_mc[2], tk_mc[3]])

        tk_z_all = {}
        tk_zev = {}
        tk_spill = {}
        tk_gspill = {}
        tk_win_free = {}
        tk_A = {}
        tk_gs_last = [None]
        prev_h = None
        win_cnt = [0]
        prev_sq = None
        prev_B7 = None
        prev_conv = None
        prev_stash = None
        prev_pw = None
        last_spill = {}

        for j in range(nch):
            pp = j % 2
            xw = [prev_h]
            tk_x = P.dma("gpsimd", "x",
                         [(t_x[g][:], i_xT[g * 128:(g + 1) * 128, j * ch:j * ch + ch + 3])
                          for g in range(DG)], waits=xw)

            for pk in range(34):
                p0 = win_cnt[0]
                win_cnt[0] += 1
                ii = pk * 4
                src_ap = i_win[ii * 128:(ii + 4) * 128, 0:128]
                dst = t_win[p0 % 4]
                inap = src_ap.rearrange("(b p) c -> p b c", b=4)
                P.dma("sync", "wn%d" % (p0 % 4), [(dst[:].rearrange("p (b c) -> p b c", b=4),
                                                   inap)],
                      waits=[tk_small] if p0 < 4 else [Tok(P.csem("win"), 4 * (p0 - 4) + 4)])

            def _sqs(eng):
                for g in range(DG):
                    eng.activation(t_sq[g][:, 0:ch + 3], t_x[g][:, 0:ch + 3], ACT.Square)
            tk_sq = P.step("scalar", _sqs, waits=[tk_x, prev_sq, tk_gspill.get(j - 1)])
            prev_sq = tk_sq

            def _sums(eng, j=j):
                for k in range(DG):
                    eng.matmul(pS1[0:1, 0:ch], t_onesc[:], t_x[k][:, 3:3 + ch],
                               start=(k == 0), stop=(k == DG - 1))
                for k in range(DG):
                    r = eng.matmul(pS2[0:1, 0:ch], t_onesc[:], t_sq[k][:, 3:3 + ch],
                                   start=(k == 0), stop=(k == DG - 1))
                if j == 0:
                    for k in range(DG):
                        eng.matmul(pR[0][0:1, 0:3], t_onesc[:], t_x[k][:, 0:3],
                                   start=(k == 0), stop=(k == DG - 1))
                    for k in range(DG):
                        r = eng.matmul(pR[1][0:1, 0:3], t_onesc[:], t_sq[k][:, 0:3],
                                       start=(k == 0), stop=(k == DG - 1))
                return r
            tk_sums = P.step("tensor", _sums, waits=[tk_x, tk_sq, prev_B7], attach=True)

            def _stats(eng, j=j):
                eng.tensor_scalar(t_stat[0:1, 0:ch], pS1[0:1, 0:ch], 1.0 / D, None, ALU.mult)
                eng.tensor_scalar(t_stat[0:1, ch:2 * ch], pS2[0:1, 0:ch], 1.0 / D, None, ALU.mult)
                if j == 0:
                    eng.tensor_scalar(t_hst[0:1, 0:3], pR[0][0:1, 0:3], 1.0 / D, None, ALU.mult)
                    eng.tensor_scalar(t_hst[0:1, 3:6], pR[1][0:1, 0:3], 1.0 / D, None, ALU.mult)
                eng.drain()
                eng.tensor_tensor(t_stat[0:1, 2 * ch:3 * ch], t_stat[0:1, 0:ch],
                                  t_stat[0:1, 0:ch], ALU.mult)
                if j == 0:
                    eng.tensor_tensor(t_hst[0:1, 6:9], t_hst[0:1, 0:3], t_hst[0:1, 0:3], ALU.mult)
                eng.drain()
                eng.tensor_tensor(t_stat[0:1, 3 * ch:4 * ch], t_stat[0:1, ch:2 * ch],
                                  t_stat[0:1, 2 * ch:3 * ch], ALU.subtract)
                if j == 0:
                    eng.tensor_tensor(t_hst[0:1, 9:12], t_hst[0:1, 3:6], t_hst[0:1, 6:9],
                                      ALU.subtract)
            tk_stats = P.step("vector", _stats, waits=[tk_sums])

            def _sqrtv(eng, j=j):
                eng.activation(t_stat[0:1, 2 * ch:3 * ch], t_stat[0:1, 3 * ch:4 * ch], ACT.Sqrt,
                               bias=t_cst[0:1, C_EPSLN:C_EPSLN + 1])
                if j == 0:
                    eng.activation(t_hst[0:1, 6:9], t_hst[0:1, 9:12], ACT.Sqrt,
                                   bias=t_cst[0:1, C_EPSLN:C_EPSLN + 1])
            tk_sqrt = P.step("scalar", _sqrtv, waits=[tk_stats])

            def _rstd(eng, j=j):
                eng.reciprocal(t_stat[0:1, ch:2 * ch], t_stat[0:1, 2 * ch:3 * ch])
                if j == 0:
                    eng.reciprocal(t_hst[0:1, 3:6], t_hst[0:1, 6:9])
            tk_rstd = P.step("vector", _rstd, waits=[tk_sqrt])

            def _reps(eng, j=j):
                eng.matmul(pR[0][0:128, 0:ch], t_ones1[:], t_stat[0:1, 0:ch], start=True, stop=True)
                r = eng.matmul(pR[1][0:128, 0:ch], t_ones1[:], t_stat[0:1, ch:2 * ch],
                               start=True, stop=True)
                if j == 0:
                    eng.matmul(pP[0][0:128, 0:3], t_ones1[:], t_hst[0:1, 0:3], start=True, stop=True)
                    r = eng.matmul(pP[1][0:128, 0:3], t_ones1[:], t_hst[0:1, 3:6],
                                   start=True, stop=True)
                return r
            tk_reps = P.step("tensor", _reps, waits=[tk_rstd], attach=True)

            def _happly(eng, j=j):
                for g in range(DG):
                    eng.tensor_tensor(t_h[g][:, 3:3 + ch], t_x[g][:, 3:3 + ch],
                                      pR[0][0:128, 0:ch], ALU.subtract)
                eng.drain()
                for g in range(DG):
                    eng.tensor_tensor(t_sq[g][:, 0:ch], t_h[g][:, 3:3 + ch],
                                      pR[1][0:128, 0:ch], ALU.mult)
                eng.drain()
                for g in range(DG):
                    eng.tensor_scalar(t_h[g][:, 3:3 + ch], t_sq[g][:, 0:ch],
                                      t_ca1[:, g:g + 1], t_mod[:, g:g + 1], ALU.mult, ALU.add)
                if j == 0:
                    for g in range(DG):
                        eng.tensor_tensor(t_h[g][:, 0:3], t_x[g][:, 0:3],
                                          pP[0][0:128, 0:3], ALU.subtract)
                    eng.drain()
                    for g in range(DG):
                        eng.tensor_tensor(t_sq[g][:, ch:ch + 3], t_h[g][:, 0:3],
                                          pP[1][0:128, 0:3], ALU.mult)
                    eng.drain()
                    for g in range(DG):
                        eng.tensor_scalar(t_sq[g][:, ch:ch + 3], t_sq[g][:, ch:ch + 3],
                                          t_ca1[:, g:g + 1], t_mod[:, g:g + 1], ALU.mult, ALU.add)
                    eng.drain()
                    for g in range(DG):
                        eng.tensor_tensor(t_h[g][:, 0:3], t_sq[g][:, ch:ch + 3], t_hmask[:],
                                          ALU.mult)
            tk_h = P.step("vector", _happly, waits=[tk_reps])
            prev_h = tk_h

            zev_base = P.vals["scalar"]

            def _zmm(eng, bump, j=j, base=j * ZG * 8, zb=zev_base):
                for mz in range(ZG):
                    zsz = _zsz(mz)
                    if mz >= 2:
                        eng.wait_ge(P.sems["scalar"], zb + mz - 1)
                    for k in range(8):
                        i = base + mz * 8 + k
                        li = mz * 8 + k
                        pk = j * 34 + li // 4
                        sl = t_win[pk % 4][:, 128 * (li % 4):128 * (li % 4) + zsz]
                        eng.wait_ge(P.dma_sem("wn%d" % (pk % 4)), 16 * (pk // 4 + 1))
                        r = eng.matmul(pZ[mz % 2][0:zsz, 0:ch], sl,
                                       t_h[k][:, 3:3 + ch], start=(k == 0), stop=(k == 7))
                        if j == 0:
                            if k == 0 and mz >= 1:
                                pass
                            r = eng.matmul(pS1[0:zsz, 4:7], t_win[i % 8][:, 0:zsz],
                                           t_h[k][:, 0:3], start=(k == 0), stop=(k == 7))
                        r.then_inc(P.csem("win"), 1)
                    tk_z_all[(j, mz)] = Tok(P.csem("win"), base + mz * 8 + 8)
            zmm_waits = [tk_h, tk_zev.get(j - 1)]
            if j == 0:
                def _zmm0(eng, bump, j=j, base=0, zb=zev_base):
                    for mz in range(ZG):
                        zsz = _zsz(mz)
                        if mz >= 1:
                            eng.wait_ge(P.sems["scalar"], zb + mz)
                        for k in range(8):
                            li = mz * 8 + k
                            pk = li // 4
                            sl = t_win[pk % 4][:, 128 * (li % 4):128 * (li % 4) + zsz]
                            eng.wait_ge(P.dma_sem("wn%d" % (pk % 4)), 16 * (pk // 4 + 1))
                            r = eng.matmul(pZ[mz % 2][0:zsz, 0:ch], sl,
                                           t_h[k][:, 3:3 + ch], start=(k == 0), stop=(k == 7))
                            r = eng.matmul(pS1[0:zsz, 4:7], sl,
                                           t_h[k][:, 0:3], start=(k == 0), stop=(k == 7))
                            r.then_inc(P.csem("win"), 1)
                        tk_z_all[(j, mz)] = Tok(P.csem("win"), mz * 8 + 8)
                P.raw("tensor", _zmm0, waits=zmm_waits)
            else:
                P.raw("tensor", _zmm, waits=zmm_waits)
            tk_z_all[j] = Tok(P.csem("win"), ZG * 8 * (j + 1))

            def _zev_raw(eng, bump, j=j):
                for mz in range(ZG):
                    zsz = _zsz(mz)
                    t = tk_z_all[(j, mz)]
                    eng.wait_ge(t.sem, t.val)
                    r = eng.activation(t_z[mz][0:zsz, 3:3 + ch], pZ[mz % 2][0:zsz, 0:ch],
                                       ACT.Copy)
                    if j == 0:
                        r = eng.activation(t_z[mz][0:zsz, 0:3], pS1[0:zsz, 4:7], ACT.Copy)
                    r.then_inc(P.sems["scalar"], 1)
                    bump()
            P.raw("scalar", _zev_raw, waits=[prev_conv, prev_stash])
            tk_zev[j] = Tok(P.sems["scalar"], P.vals["scalar"])

            zev0 = tk_zev[j].val - ZG
            tk_convp = {}
            for mz0 in [8, 10, 12, 14, 16, 0, 2, 4, 6]:
                pair = [mz0] + ([mz0 + 1] if mz0 + 1 < ZG else [])

                def _convp(eng, pair=pair, mz0=mz0, j=j):
                    if j > 0:
                        for mz in pair:
                            eng.tensor_copy(t_z[mz][0:_zsz(mz), 0:3],
                                            t_zst[0:_zsz(mz), 3 * mz:3 * mz + 3])
                        eng.drain()
                    for mz in pair:
                        eng.tensor_copy(t_zst[0:_zsz(mz), 3 * mz:3 * mz + 3],
                                        t_z[mz][0:_zsz(mz), ch:ch + 3])
                    eng.drain()
                    s0 = {mz0: t_e, mz0 + 1: t_ta[1]}
                    s1 = {mz0: t_f, mz0 + 1: t_ta[0]}
                    for mz in pair:
                        eng.tensor_scalar(s0[mz][0:_zsz(mz), 0:ch], t_z[mz][0:_zsz(mz), 0:ch],
                                          t_wcv[0:_zsz(mz), 0 * ZG + mz:0 * ZG + mz + 1],
                                          t_bcv[0:_zsz(mz), mz:mz + 1], ALU.mult, ALU.add)
                    eng.drain()
                    for mz in pair:
                        eng.scalar_tensor_tensor(s1[mz][0:_zsz(mz), 0:ch],
                                                 t_z[mz][0:_zsz(mz), 1:1 + ch],
                                                 t_wcv[0:_zsz(mz), 1 * ZG + mz:1 * ZG + mz + 1],
                                                 s0[mz][0:_zsz(mz), 0:ch], ALU.mult, ALU.add)
                    eng.drain()
                    for mz in pair:
                        eng.scalar_tensor_tensor(s0[mz][0:_zsz(mz), 0:ch],
                                                 t_z[mz][0:_zsz(mz), 2:2 + ch],
                                                 t_wcv[0:_zsz(mz), 2 * ZG + mz:2 * ZG + mz + 1],
                                                 s1[mz][0:_zsz(mz), 0:ch], ALU.mult, ALU.add)
                    eng.drain()
                    for mz in pair:
                        eng.scalar_tensor_tensor(t_cz[mz][0:_zsz(mz), 0:ch],
                                                 t_z[mz][0:_zsz(mz), 3:3 + ch],
                                                 t_wcv[0:_zsz(mz), 3 * ZG + mz:3 * ZG + mz + 1],
                                                 s0[mz][0:_zsz(mz), 0:ch], ALU.mult, ALU.add)
                w = ([Tok(P.sems["scalar"], zev0 + pair[-1] + 1)] if mz0 >= 8
                     else [tk_zev[j]])
                tk_convp[mz0] = P.step("vector", _convp, waits=w)
            tk_conv = tk_convp[6]
            prev_conv = tk_conv
            prev_stash = tk_conv

            def _gsilu(eng):
                for g in range(8):
                    eng.activation(t_gate[g][:, 0:ch], t_cz[8 + g][:, 0:ch], ACT.Silu)
            tk_gs = P.step("scalar", _gsilu, waits=[tk_convp[14], tk_gspill.get(j - 1)])
            tk_gs_last[0] = tk_gspill.get(j - 1)
            tk_gspill[j] = P.dma("gpsimd", "gsp",
                                 [(o_gate[g * 128:(g + 1) * 128, j * ch:(j + 1) * ch],
                                   t_gate[g][:, 0:ch]) for g in range(8)], waits=[tk_gs])
            if "cz" in dbg:
                P.dma("gpsimd", "dbg",
                      [(dbg["cz"][mz * 128:mz * 128 + _zsz(mz), j * ch:(j + 1) * ch],
                        t_cz[mz][0:_zsz(mz), 0:ch]) for mz in range(ZG)], waits=[tk_conv])
            if "h" in dbg:
                P.dma("gpsimd", "dbg",
                      [(dbg["h"][g * 128:(g + 1) * 128, j * ch:(j + 1) * ch],
                        t_h[g][:, 3:3 + ch]) for g in range(DG)],
                      waits=[tk_h, tk_z_all[j]])

            def _pw_v(eng, j=j):
                eng.tensor_scalar(t_db[:], t_tb[:, 0:1], float(j * ch), None, ALU.add)
                eng.drain()
                eng.tensor_tensor(t_db2[:], t_nsp[:], t_db[:], ALU.mult)
            tk_pwv = P.step("vector", _pw_v, waits=[tk_convp[16], prev_pw])

            def _tw_s(eng):
                eng.activation(t_tw[0:32, 0:ch], t_iota[:], ACT.Exp,
                               scale=t_nsp[0:32, 0:1], bias=t_db2[:])
            tk_tws = P.step("scalar", _tw_s, waits=[tk_pwv])

            def _p_v(eng):
                eng.tensor_scalar(t_pwb[0:32, 0:ch], t_cz[16][0:32, 0:ch],
                                  t_score[:], 20.0, ALU.mult, ALU.min)
                eng.drain()
                eng.tensor_scalar(t_pwa[0:32, 0:ch], t_pwb[0:32, 0:ch], -20.0, None, ALU.max)
            tk_pv = P.step("vector", _p_v, waits=[tk_convp[16], tk_tws])

            def _p_s(eng):
                eng.activation(t_pwb[0:32, 0:ch], t_pwa[0:32, 0:ch], ACT.Exp)
            tk_ps = P.step("scalar", _p_s, waits=[tk_pv])

            def _pw_fin(eng):
                eng.tensor_tensor(t_pw[0:32, 0:ch], t_pwb[0:32, 0:ch], t_tw[0:32, 0:ch], ALU.mult)
                eng.drain()
                eng.tensor_tensor_scan(t_dencs[0:32, 0:ch], t_pw[0:32, 0:ch], t_pw[0:32, 0:ch],
                                       t_dcar[:], ALU.add, ALU.bypass)
                eng.drain()
                eng.tensor_copy(t_dcar[:], t_dencs[0:32, ch - 1:ch])
            tk_pw_t = P.step("vector", _pw_fin, waits=[tk_ps])
            prev_pw = P.dma("gpsimd", "dsp",
                            [(o_den[0:32, j * ch:(j + 1) * ch], t_dencs[0:32, 0:ch])],
                            waits=[tk_pw_t])
            if "pw" in dbg:
                P.dma("gpsimd", "dbg", [(dbg["pw"][0:32, j * ch:(j + 1) * ch],
                                         t_pw[0:32, 0:ch])], waits=[tk_pw_t])

            tk_S = {}
            tk_B = {}

            def emit_prep(g, j=j):
                def _pr(eng, g=g):
                    return eng.matmul(pP[g % 2][0:128, 0:ch],
                                      t_r32[0:32, g * 128:(g + 1) * 128],
                                      t_pw[0:32, 0:ch], start=True, stop=True)
                w = [tk_pw_t]
                if j == 0 and g < 2:
                    w.append(tk_h)
                else:
                    w.append(tk_A.get((g - 2) % 8))
                return P.step("tensor", _pr, waits=w, attach=True)

            def emit_A(g, tk_prep_g, j=j):
                def _A(eng, g=g):
                    xv = t_cz[g][0:128, 0:ch]
                    for m in range(MTH):
                        gm = g * 4 + m
                        eng.tensor_scalar(t_ta[m][:, 0:ch], xv, t_invP[:, gm:gm + 1], None,
                                          ALU.mult)
                    eng.tensor_tensor(t_e[:, 0:ch], xv, xv, ALU.mult)
                    eng.drain()
                    eng.tensor_copy(t_ki[:, 0:ch], t_ta[0][:, 0:ch])
                    eng.tensor_copy(t_ki2[:, 0:ch], t_ta[1][:, 0:ch])
                    eng.tensor_scalar(t_f[:, 0:ch], t_e[:, 0:ch], t_wpoly[:, 2:3], None, ALU.mult)
                    eng.drain()
                    eng.scalar_tensor_tensor(t_ta[0][:, 0:ch], t_ki[:, 0:ch],
                                             t_negP[:, g * 4:g * 4 + 1], xv, ALU.mult, ALU.add)
                    eng.scalar_tensor_tensor(t_ta[1][:, 0:ch], t_ki2[:, 0:ch],
                                             t_negP[:, g * 4 + 1:g * 4 + 2], xv,
                                             ALU.mult, ALU.add)
                    eng.scalar_tensor_tensor(t_e[:, 0:ch], xv, t_wpoly[:, 1:2], t_f[:, 0:ch],
                                             ALU.mult, ALU.subtract)
                    eng.drain()
                    eng.tensor_copy(t_ki[:, 0:ch], t_ta[2][:, 0:ch])
                    eng.tensor_copy(t_ki2[:, 0:ch], t_ta[3][:, 0:ch])
                    eng.tensor_scalar(t_f[:, 0:ch], t_e[:, 0:ch], t_wpoly[:, 0:1], None,
                                      ALU.add)
                    eng.drain()
                    eng.scalar_tensor_tensor(t_ta[2][:, 0:ch], t_ki[:, 0:ch],
                                             t_negP[:, g * 4 + 2:g * 4 + 3], xv,
                                             ALU.mult, ALU.add)
                    eng.scalar_tensor_tensor(t_ta[3][:, 0:ch], t_ki2[:, 0:ch],
                                             t_negP[:, g * 4 + 3:g * 4 + 4], xv,
                                             ALU.mult, ALU.add)
                    eng.tensor_tensor(t_pp[g % 2][:, 0:ch], t_f[:, 0:ch],
                                      pP[g % 2][0:128, 0:ch], ALU.mult)
                    eng.drain()
                return P.step("vector", _A, waits=[tk_convp[g & ~1], tk_prep_g, tk_S.get(g - 1)])

            def emit_S(g):
                def _S(eng, g=g):
                    for m in range(MTH):
                        gm = g * 4 + m
                        eng.activation(t_s4[m][:, 0:ch], t_ta[m][:, 0:ch], ACT.Sin,
                                       scale=t_th4[:, gm:gm + 1])
                        eng.activation(t_s8[m][:, 0:ch], t_ta[m][:, 0:ch], ACT.Sin,
                                       scale=t_th4[:, gm:gm + 1], bias=t_pih[:])
                    eng.drain()
                    for m in range(MTH):
                        eng.activation(t_mg[2 * m + 1][:, 0:ch], t_s4[m][:, 0:ch],
                                       ACT.Square)
                return P.step("scalar", _S, waits=[tk_A[g], tk_B.get(g - 1)])

            def emit_B(g, j=j):
                def _B(eng, g=g):
                    PPg = t_pp[g % 2]
                    for m in range(MTH):
                        eng.tensor_scalar(t_cs[2 * m][:, 0:ch], t_mg[2 * m + 1][:, 0:ch],
                                          -2.0, 1.0, ALU.mult, ALU.add)
                        eng.tensor_tensor(t_mg[2 * m][:, 0:ch], t_s4[m][:, 0:ch],
                                          t_s8[m][:, 0:ch], ALU.mult)
                    eng.drain()
                    for m in range(MTH):
                        eng.tensor_tensor(t_s4[m][:, 0:ch], t_mg[2 * m][:, 0:ch],
                                          t_cs[2 * m][:, 0:ch], ALU.mult)
                        eng.tensor_tensor(t_s8[m][:, 0:ch], t_mg[2 * m][:, 0:ch],
                                          t_mg[2 * m][:, 0:ch], ALU.mult)
                    eng.drain()
                    for m in range(MTH):
                        eng.scalar_tensor_tensor(t_cs[2 * m][:, 0:ch], t_s8[m][:, 0:ch],
                                                 -8.0, PPg[:, 0:ch], ALU.mult, ALU.mult)
                    eng.drain()
                    for m in range(MTH):
                        eng.tensor_tensor(t_mg[2 * m][:, 0:ch], PPg[:, 0:ch],
                                          t_cs[2 * m][:, 0:ch], ALU.add)
                        eng.scalar_tensor_tensor(t_mg[2 * m + 1][:, 0:ch], t_s4[m][:, 0:ch],
                                                 4.0, PPg[:, 0:ch], ALU.mult, ALU.mult)
                    eng.drain()
                    for m in range(MTH):
                        for side in range(2):
                            i = 2 * m + side
                            col = _scol(g, m, side)
                            eng.tensor_tensor_scan(t_cs[i][:, 0:ch], t_mg[i][:, 0:ch],
                                                   t_mg[i][:, 0:ch], t_carry[:, col:col + 1],
                                                   ALU.add, ALU.bypass)
                    eng.drain()
                    for m in range(MTH):
                        for side in range(2):
                            i = 2 * m + side
                            col = _scol(g, m, side)
                            eng.tensor_copy(t_carry[:, col:col + 1], t_cs[i][:, ch - 1:ch])
                    eng.drain()
                wl = [tk_S[g]]
                if g > 0:
                    wl.append(tk_spill.get((j, g - 1)))
                    wl.append(tk_spill.get((j, g - 2), last_spill.get(j - 1)))
                else:
                    wl.append(last_spill.get(j - 1))
                    wl.append(tk_spill.get((j - 1, 6)))
                return P.step("vector", _B, waits=wl)

            def emit_spill(g, j=j):
                trs = []
                for m in range(MTH):
                    for side in range(2):
                        i = 2 * m + side
                        rb = _scol(g, m, side) * 128
                        trs.append((o_cs[rb:rb + 128, j * ch:(j + 1) * ch], t_cs[i][:, 0:ch]))
                tk_spill[(j, g)] = P.dma("gpsimd", "sp%d" % (g % 2), trs, waits=[tk_B[g]])
                if "mg" in dbg:
                    trs2 = []
                    for m in range(MTH):
                        for side in range(2):
                            i = 2 * m + side
                            rb = _scol(g, m, side) * 128
                            trs2.append((dbg["mg"][rb:rb + 128, j * ch:(j + 1) * ch],
                                         t_mg[i][:, 0:ch]))
                    P.dma("gpsimd", "dbg", trs2, waits=[tk_B[g]])

            for g in range(8):
                pr = emit_prep(g)
                tk_A[g] = emit_A(g, pr)
                if g >= 1:
                    tk_B[g - 1] = emit_B(g - 1)
                    emit_spill(g - 1)
                tk_S[g] = emit_S(g)
            tk_B[7] = emit_B(7)
            emit_spill(7)
            last_spill[j] = tk_spill[(j, 7)]
            prev_B7 = tk_B[7]

        def _sout(eng):
            eng.memzero(t_sfin[:, 0:66])
            eng.drain()
            eng.tensor_copy(t_sfin[:, 0:NS], t_carry[:])
            eng.drain()
            eng.tensor_copy(t_sfin[0:32, 64:65], t_dcar[:])
        tk_fin = P.step("vector", _sout, waits=[prev_B7, prev_pw])
        if F is None:
            P.dma("sync", "store", [(o_s, t_sfin[:, 0:66])], waits=[tk_fin])

            def _bar(eng):
                for s in ("sp0", "sp1", "gsp", "dsp", "dbg", "store"):
                    if s in P.dsems:
                        eng.wait_ge(P.dma_sem(s), P.dvals[s])
            P.step("sync", _bar)
        else:
            def _slots(eng):
                for i in range(NCORE):
                    eng.tensor_scalar(t_sall8[:, 66 * i:66 * (i + 1)], t_sfin[:, 0:66],
                                      t_wsl[:, i:i + 1], None, ALU.mult)
            tk_slots = P.step("vector", _slots, waits=[tk_fin])
            P.dma("gpsimd", "ccin",
                  [(F["cc_in"].ap()[i * 128:(i + 1) * 128, 0:66],
                    t_sall8[:, 66 * i:66 * (i + 1)]) for i in range(NCORE)]
                  ,
                  waits=[tk_slots])

            def _barg(eng):
                for s in ("sp0", "sp1", "gsp", "dsp", "dbg", "x", "ccin"):
                    if s in P.dsems:
                        eng.wait_ge(P.dma_sem(s), P.dvals[s])
            bt = [P.step("gpsimd", _barg)]

            def _bars(eng):
                for s in P.dvals:
                    if s.startswith(("wn", "wm", "mc", "fesmall")):
                        eng.wait_ge(P.dma_sem(s), P.dvals[s])
            bt.append(P.step("sync", _bars))
            for e in ("tensor", "vector", "scalar"):
                bt.append(P.step(e, lambda eng: None))
            F["fe_bar"] = bt

    return nc



def build_be(T_core, ch, debug=(), F=None):
    nch = T_core // ch
    if F is None:
        nc = bass.Bass("TRN2", target_bir_lowering=False)
        i_xT = nc.dram_tensor("xT", [D, T_core + 3], F32, kind="ExternalInput").ap()
        i_cs = nc.dram_tensor("sp_cs", [NS * 128, T_core], F32, kind="ExternalInput").ap()
        i_gate = nc.dram_tensor("sp_gate", [8 * 128, T_core], F32, kind="ExternalInput").ap()
        i_den = nc.dram_tensor("sp_den", [32, T_core], F32, kind="ExternalInput").ap()
        i_sall = nc.dram_tensor("s_all", [NCORE * 128, 66], F32, kind="ExternalInput").ap()
        i_modrow = nc.dram_tensor("mod_row", [1, 6 * D], F32, kind="ExternalInput").ap()
    else:
        nc = F["nc"]
        i_xT = F["i_xT"]
        i_cs = F["sp_cs"]
        i_gate = F["sp_gate"]
        i_den = F["sp_den"]
        i_sall = F["cc_out"].ap()
    i_smask = nc.dram_tensor("smask", [128, NCORE], F32, kind="ExternalInput").ap()
    i_w1 = nc.dram_tensor("w1t", [FG * 8 * 128, 128], F32, kind="ExternalInput").ap()
    i_w2 = nc.dram_tensor("w2t", [8 * FG * 128, 128], F32, kind="ExternalInput").ap()
    i_wo = nc.dram_tensor("wot", [8 * 8 * 128, 128], F32, kind="ExternalInput").ap()
    i_wre = nc.dram_tensor("wrebd", [8 * 128, 128], F32, kind="ExternalInput").ap()
    i_nsc = nc.dram_tensor("nsccols", [128, 8], F32, kind="ExternalInput").ap()
    i_obd = nc.dram_tensor("obd32", [128, 8 * 32], F32, kind="ExternalInput").ap()
    i_b1 = nc.dram_tensor("b1c", [128, FG], F32, kind="ExternalInput").ap()
    i_b2 = nc.dram_tensor("b2c", [128, 8], F32, kind="ExternalInput").ap()
    if F is None:
        i_r32 = nc.dram_tensor("r32", [32, 8 * 128], F32, kind="ExternalInput").ap()
        i_ones1 = nc.dram_tensor("ones_k1", [1, 128], F32, kind="ExternalInput").ap()
        i_onesc = nc.dram_tensor("ones_col", [128, 1], F32, kind="ExternalInput").ap()
        i_cst = nc.dram_tensor("cst", [128, NCST], F32, kind="ExternalInput").ap()

    o_y = nc.dram_tensor("yT", [D, T_core], F32, kind="ExternalOutput").ap()
    dbg_shapes = {"y": [8 * 128, T_core], "att": [8 * 128, T_core], "x1": [DG * 128, T_core],
                  "h2": [DG * 128, T_core], "ffg": [FG * 128, T_core], "sk": [32, T_core],
                  "inv": [32, T_core], "scar": [128, 66], "ycar": [128, 8],
                  "scr2": [128, 66], "smk": [128, NCORE], "scar0": [128, 66]}
    dbg = {nm: nc.dram_tensor("dbg_" + nm, dbg_shapes[nm], F32, kind="ExternalOutput").ap()
           for nm in debug}

    es = contextlib.ExitStack()
    with es:
        def sb(name, shape, dt=F32):
            if F is not None:
                name = "b_" + name
            return es.enter_context(nc.sbuf_tensor(name, shape, dt))

        def pt(name):
            if F is not None:
                name = "b_" + name
            return es.enter_context(nc.psum_tensor(name, [128, 512], F32))

        t_mod = sb("t_mod", [128, 48]) if F is None else F["t_mod"]
        t_cb1 = sb("t_cb1", [128, 8])
        t_ga = sb("t_ga", [128, 8])
        t_gb256 = sb("t_gb256", [128, 8])
        t_gbfix = sb("t_gbfix", [128, 8])
        t_scar = sb("t_scar", [128, 66])
        t_scr2 = sb("t_scr2", [128, 66])
        t_smk = sb("t_smk", [128, NCORE])
        t_dcar = sb("t_dcar", [32, 1])
        t_ycar = sb("t_ycar", [128, 8])
        t_wre = [sb(f"t_wre{i}", [128, 128]) for i in range(8)]
        t_nsc = sb("t_nsc", [128, 8])
        t_obd = sb("t_obd", [128, 8 * 32])
        t_b1 = sb("t_b1", [128, FG])
        t_bias1 = sb("t_bias1", [128, FG])
        t_b2p = sb("t_b2p", [128, 8])
        t_bias2 = sb("t_bias2", [128, 8])
        if F is None:
            t_r32 = sb("t_r32", [32, 8 * 128])
            t_ones1 = sb("t_ones1", [1, 128])
            t_onesc = sb("t_onesc", [128, 1])
            t_cst = sb("t_cst", [128, NCST])
            t_modrow = sb("t_modrow", [1, 6 * D])
        else:
            t_r32 = F["t_r32"]
            t_ones1 = F["t_ones1"]
            t_onesc = F["t_onesc"]
            t_cst = F["t_cst"]
        t_stat = sb("t_stat", [1, 4 * ch])

        t_x = [sb(f"t_x{g}", [128, ch]) for g in range(DG)]
        t_csr = [sb(f"t_csr{i}", [128, ch]) for i in range(16)]
        t_sqg = [sb(f"t_sqg{i}", [128, ch]) for i in range(8)]
        t_y = [sb(f"t_y{g}", [128, ch]) for g in range(8)]
        t_gt = [sb(f"t_gt{g}", [128, ch]) for g in range(8)]
        t_x1 = [sb(f"t_x1{g}", [128, ch]) for g in range(DG)]
        t_gfp = t_csr + t_sqg + t_gt
        t_den = sb("t_den", [32, ch])
        t_inv = sb("t_inv", [32, ch])
        t_sk = sb("t_sk", [32, ch])
        t_ssc = sb("t_ssc", [32, ch])
        t_e = sb("t_e", [128, ch])
        t_f = sb("t_f", [128, ch])
        t_g2 = sb("t_g2", [128, ch])
        t_gg = sb("t_gg", [128, ch])
        t_h3 = sb("t_h3", [128, ch])
        t_n1 = sb("t_n1", [128, ch], I32)
        t_m2i = sb("t_m2i", [128, ch], I32)
        t_gq = sb("t_gq", [128, ch], I32)
        _lfp = ([t_e, t_f, t_g2, t_gg, t_h3]
                + [sb(f"t_gl{i}", [128, ch]) for i in range(11)])
        _li32 = [t_n1, t_m2i, t_gq, sb("t_gl_i", [128, ch], I32)]
        GL = [dict(n1f=_lfp[4 * L + 0], e=_lfp[4 * L + 1], f=_lfp[4 * L + 2],
                   gg=_lfp[4 * L + 3], tmp=_li32[L])
              for L in range(4)]
        t_wb1 = [sb(f"t_wb1_{i}", [128, 512]) for i in range(4)]
        t_wb2 = [sb(f"t_wb2_{i}", [128, 512]) for i in range(4)]
        t_wbo = [sb(f"t_wbo_{i}", [128, 512]) for i in range(4)]

        pA = [pt("pA0"), pt("pA1")]
        pB = [pt("pB0"), pt("pB1")]
        pSS = pt("pSS")
        pS1 = pt("pS1")
        pS2 = pt("pS2")
        pRp = pt("pRp")

        if F is None:
            block = es.enter_context(nc.Block())
            P = Prog(nc, block, es)
            bar = []
        else:
            block = F["block"]
            P = F["P"]
            bar = F["fe_bar"]
            for e in Prog.ENGINES:
                P.step(e, lambda eng: None, waits=bar)

        ld0 = [
            (t_smk[:], i_smask),
            (t_nsc[:], i_nsc), (t_obd[:], i_obd), (t_b1[:], i_b1), (t_b2p[:], i_b2),
        ] + [(t_wre[i][:], i_wre[i * 128:(i + 1) * 128, :]) for i in range(8)]
        if F is None:
            ld0 += [(t_r32[:], i_r32), (t_ones1[:], i_ones1), (t_onesc[:], i_onesc),
                    (t_cst[:], i_cst), (t_modrow[:], i_modrow)]
        tk_small = P.dma("sync", "besmall", ld0,
                         waits=[F["cc_done"]] if F is not None else [])
        tk_s0 = P.dma("gpsimd" if F is not None else "sync", "scl",
                      [(t_scr2[:], i_sall[0:128, 0:66])],
                      waits=[F["cc_done"]] if F is not None else [])

        if F is None:
            mod_tr = []
            for v6 in range(6):
                for k in range(8):
                    mod_tr.append((t_mod[:, v6 * 8 + k:v6 * 8 + k + 1],
                                   t_modrow[0:1, v6 * D + k * 128:v6 * D + (k + 1) * 128],
                                   dict(allow_slow_non_contiguous=True)))
            tk_modc = P.dma("sync", "besmall", mod_tr, waits=[tk_small])
        else:
            tk_modc = tk_small

        if "scr2" in dbg:
            P.dma("gpsimd", "dbg", [(dbg["scr2"], t_scr2[:])], waits=[tk_small])
        if "smk" in dbg:
            P.dma("gpsimd", "dbg", [(dbg["smk"], t_smk[:])], waits=[tk_small])

        def _scar0(eng):
            if "scr2" in dbg or "smk" in dbg:
                eng.wait_ge(P.dma_sem("dbg"), P.dvals["dbg"])
            eng.tensor_scalar(t_scar[:], t_scr2[:], t_smk[:, 0:1], None, ALU.mult)
        tk_sc = P.step("vector", _scar0, waits=[tk_small, tk_s0])
        if "scar0" in dbg:
            P.dma("gpsimd", "dbg2", [(dbg["scar0"], t_scar[:])], waits=[tk_sc])
            def _w0(eng):
                eng.wait_ge(P.dma_sem("dbg2"), P.dvals["dbg2"])
            tk_sc = P.step("vector", _w0, waits=[tk_sc])
        for i in range(1, NCORE):
            tk_l = P.dma("gpsimd" if F is not None else "sync", "scl",
                         [(t_scr2[:], i_sall[i * 128:(i + 1) * 128, 0:66])],
                         waits=[tk_sc])

            def _sacc(eng, i=i):
                eng.scalar_tensor_tensor(t_scar[:], t_scr2[:], t_smk[:, i:i + 1], t_scar[:],
                                         ALU.mult, ALU.add)
            tk_sc = P.step("vector", _sacc, waits=[tk_l])

        def _postcar(eng):
            eng.tensor_copy(t_dcar[:], t_scar[0:32, 64:65])
            eng.tensor_scalar(t_cb1[:], t_mod[:, 32:40], 1.0, None, ALU.add)
            for src, dst in ((16, t_ga), (40, t_gbfix)):
                eng.tensor_scalar(t_e[:, 0:8], t_mod[:, src:src + 8], 0.0, None, ALU.is_ge)
                eng.drain()
                eng.tensor_scalar(t_f[:, 0:8], t_e[:, 0:8], 2.0, -1.0, ALU.mult, ALU.add)
                eng.tensor_scalar(t_e[:, 8:16], t_mod[:, src:src + 8], -1.0, None, ALU.mult)
                eng.drain()
                eng.tensor_tensor(t_e[:, 0:8], t_e[:, 8:16], t_mod[:, src:src + 8], ALU.max)
                eng.drain()
                eng.tensor_scalar(t_e[:, 0:8], t_e[:, 0:8], 0.25, None, ALU.max)
                eng.drain()
                eng.tensor_tensor(dst[:], t_f[:, 0:8], t_e[:, 0:8], ALU.mult)
                eng.drain()
            eng.tensor_scalar(t_gb256[:], t_gbfix[:], 1.0 / 256.0, None, ALU.mult)
            eng.drain()
            eng.tensor_scalar(t_gbfix[:], t_gbfix[:], -128.0, None, ALU.mult)
            eng.tensor_scalar(t_bias1[:], t_b1[:], 256.0, OFF15 + 0.5, ALU.mult, ALU.add)
            eng.tensor_scalar(t_e[:, 0:8], t_b2p[:], 256.0, OFF15 + 0.5, ALU.mult, ALU.add)
            eng.drain()
            eng.tensor_copy(t_b2p[:], t_e[:, 0:8])
            for i in range(8):
                eng.drain()
                eng.tensor_scalar(t_wb1[0][:, 0:128], t_wre[i][:], t_nsc[:, i:i + 1],
                                  None, ALU.mult)
                eng.drain()
                eng.tensor_copy(t_wre[i][:], t_wb1[0][:, 0:128])
        tk_prep = P.step("vector", _postcar, waits=[tk_sc, tk_modc])

        def _ycar_mm(eng):
            r = None
            for g in range(8):
                for i in range(8):
                    m, side = i // 2, i % 2
                    col = _scol(g, m, side)
                    r = eng.matmul(pA[0][0:128, g:g + 1], t_wre[i][:],
                                   t_scar[:, col:col + 1],
                                   start=(g == 0 and i == 0), stop=(g == 7 and i == 7))
            return r
        tk_ycmm = P.step("tensor", _ycar_mm, waits=[tk_prep], attach=True)

        def _ycar_ev(eng):
            eng.tensor_copy(t_ycar[:], pA[0][0:128, 0:8])
        tk_ycar = P.step("vector", _ycar_ev, waits=[tk_ycmm])
        if "scar" in dbg:
            P.dma("gpsimd", "dbg", [(dbg["scar"], t_scar[:])], waits=[tk_prep])
        if "ycar" in dbg:
            P.dma("gpsimd", "dbg", [(dbg["ycar"], t_ycar[:])], waits=[tk_ycar])

        w1_cnt = [0]
        w2_cnt = [0]
        wo_cnt = [0]
        tk_w1_free = {}
        tk_w2_free = {}
        tk_wo_free = {}
        tk_store = {}
        tk_b2 = {}
        tk_bias2_done = None
        prev_sq = {}
        prev_mm2 = tk_ycar
        prev_ffB = None
        prev_den = None
        prev_h2 = None

        for j in range(nch):
            tk_x = P.dma("gpsimd", "x",
                         [(t_x[g][:, 0:ch], i_xT[g * 128:(g + 1) * 128, j * ch + 3:j * ch + 3 + ch])
                          for g in range(DG)], waits=[tk_store.get(j - 1), tk_ycar])

            tk_dl = P.dma("gpsimd", "den",
                          [(t_den[0:32, 0:ch], i_den[0:32, j * ch:(j + 1) * ch])],
                          waits=[prev_den])

            def _den(eng):
                eng.tensor_scalar(t_e[0:32, 0:ch], t_den[0:32, 0:ch], t_dcar[:], 1e-4,
                                  ALU.add, ALU.max)
                eng.drain()
                eng.reciprocal(t_inv[0:32, 0:ch], t_e[0:32, 0:ch])
            tk_inv = P.step("vector", _den, waits=[tk_dl, tk_prep, tk_ycar])
            prev_den = tk_inv
            if "inv" in dbg:
                P.dma("gpsimd", "dbg", [(dbg["inv"][0:32, j * ch:(j + 1) * ch],
                                           t_inv[0:32, 0:ch])], waits=[tk_inv])

            tk_yev = {}
            tk_ssg = {}
            tk_csr = {}
            tk_ymm = {}
            for g in range(8):
                cset = t_csr[8 * (g % 2):8 * (g % 2) + 8]
                trs = []
                for m in range(MTH):
                    for side in range(2):
                        i = 2 * m + side
                        rb = _scol(g, m, side) * 128
                        trs.append((cset[i][:, 0:ch], i_cs[rb:rb + 128, j * ch:(j + 1) * ch]))
                pw_ = ([prev_sq.get((j, g - 2)), tk_ymm.get((j, g - 2))] if g >= 2 else
                       [prev_sq.get((j - 1, g + 6)), tk_ymm.get((j - 1, g + 6))])
                tk_csr[g] = P.dma("sync", "cs%d" % (g % 2), trs,
                                  waits=pw_ + [prev_mm2, tk_small])

                def _sqs(eng, g=g, cset=cset):
                    for i in range(8):
                        m, side = i // 2, i % 2
                        col = _scol(g, m, side)
                        eng.activation(t_sqg[i][:, 0:ch], cset[i][:, 0:ch], ACT.Square,
                                       bias=t_scar[:, col:col + 1])
                tk_sq = P.step("scalar", _sqs,
                               waits=[tk_csr[g], tk_ssg.get(g - 1), prev_ffB])
                prev_sq[(j, g)] = tk_sq

                def _ssmm(eng, g=g):
                    r = None
                    for i in range(8):
                        r = eng.matmul(pSS[0:32, 0:ch], t_obd[:, g * 32:(g + 1) * 32],
                                       t_sqg[i][:, 0:ch], start=(g == 0 and i == 0),
                                       stop=(g == 7 and i == 7))
                    return r
                tk_ssg[g] = P.step("tensor", _ssmm, waits=[tk_sq, tk_inv if g == 0 else None],
                                   attach=True)

                def _ymm(eng, g=g, cset=cset):
                    r = None
                    for i in range(8):
                        r = eng.matmul(pA[g % 2][0:128, 0:ch], t_wre[i][:], cset[i][:, 0:ch],
                                       start=(i == 0), stop=(i == 7))
                    return r
                tk_ym = P.step("tensor", _ymm,
                               waits=[tk_csr[g], tk_yev.get(g - 2, tk_ycar if g < 2 else None)],
                               attach=True)
                tk_ymm[(j, g)] = tk_ym

                def _yev(eng, g=g):
                    eng.activation(t_y[g][:, 0:ch], pA[g % 2][0:128, 0:ch], ACT.Copy)
                tk_yev[g] = P.step("scalar", _yev, waits=[tk_ym, prev_h2])

            def _sscale(eng):
                eng.tensor_tensor(t_ssc[0:32, 0:ch], t_inv[0:32, 0:ch], t_inv[0:32, 0:ch],
                                  ALU.mult)
                eng.drain()
                eng.scalar_tensor_tensor(t_sk[0:32, 0:ch], t_ssc[0:32, 0:ch], 1.0 / 256.0,
                                         pSS[0:32, 0:ch], ALU.mult, ALU.mult)
            tk_ms = P.step("vector", _sscale, waits=[tk_ssg[7], tk_inv])

            def _srt(eng):
                eng.activation(t_ssc[0:32, 0:ch], t_sk[0:32, 0:ch], ACT.Sqrt,
                               bias=t_cst[0:32, C_EPSRMS:C_EPSRMS + 1])
            tk_srt = P.step("scalar", _srt, waits=[tk_ms])

            def _skv(eng):
                eng.reciprocal(t_sk[0:32, 0:ch], t_ssc[0:32, 0:ch])
                eng.drain()
                eng.tensor_tensor(t_ssc[0:32, 0:ch], t_inv[0:32, 0:ch], t_sk[0:32, 0:ch],
                                  ALU.mult)
            tk_sk = P.step("vector", _skv, waits=[tk_srt])
            if "sk" in dbg:
                P.dma("gpsimd", "dbg", [(dbg["sk"][0:32, j * ch:(j + 1) * ch],
                                           t_ssc[0:32, 0:ch])], waits=[tk_sk])

            tk_gl = P.dma("gpsimd", "gate",
                          [(t_gt[g][:, 0:ch], i_gate[g * 128:(g + 1) * 128, j * ch:(j + 1) * ch])
                           for g in range(8)], waits=[prev_mm2, tk_ycar])

            tk_att = {}
            for g in range(8):
                def _srep(eng, g=g):
                    return eng.matmul(pB[g % 2][0:128, 0:ch],
                                      t_r32[0:32, g * 128:(g + 1) * 128],
                                      t_ssc[0:32, 0:ch], start=True, stop=True)
                tk_sr = P.step("tensor", _srep,
                               waits=[tk_sk, tk_att.get(g - 2, prev_mm2 if g < 2 else None)],
                               attach=True)

                def _att(eng, g=g):
                    eng.tensor_tensor(t_e[:, 0:ch], t_gt[g][:, 0:ch], pB[g % 2][0:128, 0:ch],
                                      ALU.mult)
                    eng.drain()
                    eng.scalar_tensor_tensor(t_gt[g][:, 0:ch], t_y[g][:, 0:ch],
                                             t_ycar[:, g:g + 1], t_e[:, 0:ch], ALU.add, ALU.mult)
                tk_att[g] = P.step("vector", _att, waits=[tk_sr, tk_gl, tk_yev[g]])
            if "y" in dbg:
                P.dma("gpsimd", "dbg", [(dbg["y"][g * 128:(g + 1) * 128, j * ch:(j + 1) * ch],
                                           t_y[g][:, 0:ch]) for g in range(8)],
                      waits=[tk_yev[7]])
            if "att" in dbg:
                P.dma("gpsimd", "dbg", [(dbg["att"][g * 128:(g + 1) * 128, j * ch:(j + 1) * ch],
                                           t_gt[g][:, 0:ch]) for g in range(8)],
                      waits=[tk_att[7]])

            tk_x1 = {}
            for md in range(DG):
                base = wo_cnt[0]
                for pq in range(2):
                    pk = (base + pq * 4) // 4
                    ii = md * 8 + pq * 4 + j * 64
                    P.dma("sync", "wo%d" % (pk % 4),
                          [(t_wbo[pk % 4][:].rearrange("p (b c) -> p b c", b=4),
                            i_wo[(ii % 64) * 128:(ii % 64 + 4) * 128, :]
                            .rearrange("(b p) c -> p b c", b=4))],
                          waits=[tk_small] if pk < 4 else [Tok(P.csem("wo"), 4 * (pk - 4) + 4)])
                wo_cnt[0] += 8

                def _womm(eng, bump, md=md, base=base):
                    for k in range(8):
                        i = base + k
                        pk = i // 4
                        eng.wait_ge(P.dma_sem("wo%d" % (pk % 4)), 16 * (pk // 4 + 1))
                        r = eng.matmul(pB[md % 2][0:128, 0:ch],
                                       t_wbo[pk % 4][:, 128 * (i % 4):128 * (i % 4) + 128],
                                       t_gt[k][:, 0:ch], start=(k == 0), stop=(k == 7))
                        r.then_inc(P.csem("wo"), 1)
                wo_waits = [tk_att[7], tk_x]
                if md >= 2:
                    wo_waits.append(tk_x1[md - 2])
                else:
                    wo_waits.append(tk_att[6 + md])
                P.raw("tensor", _womm, waits=wo_waits)
                tk_womm = Tok(P.csem("wo"), base + 8)

                def _x1f(eng, md=md):
                    eng.scalar_tensor_tensor(t_x1[md][:, 0:ch], pB[md % 2][0:128, 0:ch],
                                             t_ga[:, md:md + 1], t_x[md][:, 0:ch],
                                             ALU.mult, ALU.add)
                tk_x1[md] = P.step("vector", _x1f, waits=[tk_womm])
            if "x1" in dbg:
                P.dma("gpsimd", "dbg", [(dbg["x1"][g * 128:(g + 1) * 128, j * ch:(j + 1) * ch],
                                           t_x1[g][:, 0:ch]) for g in range(DG)],
                      waits=[tk_x1[7]])

            def _sq2(eng):
                for g in range(DG):
                    eng.activation(t_sqg[g][:, 0:ch], t_x1[g][:, 0:ch], ACT.Square)
            tk_sq2 = P.step("scalar", _sq2, waits=[tk_x1[7], tk_ssg[7]])

            def _sums2(eng):
                for k in range(DG):
                    eng.matmul(pS1[0:1, 0:ch], t_onesc[:], t_x1[k][:, 0:ch],
                               start=(k == 0), stop=(k == DG - 1))
                for k in range(DG):
                    r = eng.matmul(pS2[0:1, 0:ch], t_onesc[:], t_sqg[k][:, 0:ch],
                                   start=(k == 0), stop=(k == DG - 1))
                return r
            tk_sums2 = P.step("tensor", _sums2, waits=[tk_x1[7], tk_sq2], attach=True)

            def _stats2(eng):
                eng.tensor_scalar(t_stat[0:1, 0:ch], pS1[0:1, 0:ch], 1.0 / D, None, ALU.mult)
                eng.tensor_scalar(t_stat[0:1, ch:2 * ch], pS2[0:1, 0:ch], 1.0 / D, None, ALU.mult)
                eng.drain()
                eng.tensor_tensor(t_stat[0:1, 2 * ch:3 * ch], t_stat[0:1, 0:ch],
                                  t_stat[0:1, 0:ch], ALU.mult)
                eng.drain()
                eng.tensor_tensor(t_stat[0:1, 3 * ch:4 * ch], t_stat[0:1, ch:2 * ch],
                                  t_stat[0:1, 2 * ch:3 * ch], ALU.subtract)
            tk_st2 = P.step("vector", _stats2, waits=[tk_sums2])

            def _sqrt2(eng):
                eng.activation(t_stat[0:1, 2 * ch:3 * ch], t_stat[0:1, 3 * ch:4 * ch], ACT.Sqrt,
                               bias=t_cst[0:1, C_EPSLN:C_EPSLN + 1])
            tk_sr2 = P.step("scalar", _sqrt2, waits=[tk_st2])

            def _rstd2(eng):
                eng.reciprocal(t_stat[0:1, ch:2 * ch], t_stat[0:1, 2 * ch:3 * ch])
            tk_rs2 = P.step("vector", _rstd2, waits=[tk_sr2])

            def _reps2(eng):
                eng.matmul(pRp[0:128, 0:ch], t_ones1[:], t_stat[0:1, 0:ch], start=True, stop=True)
                return eng.matmul(pSS[0:128, 0:ch], t_ones1[:], t_stat[0:1, ch:2 * ch],
                                  start=True, stop=True)
            tk_rep2 = P.step("tensor", _reps2, waits=[tk_rs2, tk_ms], attach=True)

            def _h2(eng):
                for g in range(DG):
                    eng.tensor_tensor(t_y[g][:, 0:ch], t_x1[g][:, 0:ch], pRp[0:128, 0:ch],
                                      ALU.subtract)
                eng.drain()
                for g in range(DG):
                    eng.tensor_tensor(t_sqg[g][:, 0:ch], t_y[g][:, 0:ch], pSS[0:128, 0:ch],
                                      ALU.mult)
                eng.drain()
                for g in range(DG):
                    eng.tensor_scalar(t_y[g][:, 0:ch], t_sqg[g][:, 0:ch],
                                      t_cb1[:, g:g + 1], t_mod[:, 24 + g:24 + g + 1],
                                      ALU.mult, ALU.add)
            tk_h2 = P.step("vector", _h2, waits=[tk_rep2, tk_yev[7]])
            prev_h2 = tk_h2
            if "h2" in dbg:
                P.dma("gpsimd", "dbg", [(dbg["h2"][g * 128:(g + 1) * 128, j * ch:(j + 1) * ch],
                                           t_y[g][:, 0:ch]) for g in range(DG)],
                      waits=[tk_h2])

            tk_pv1 = {}
            tk_pv1a = {}
            tk_pgs = {}
            tk_pv2 = {}
            tk_mm1p = {}
            NP = FG // 2

            def emit_pv2(p):
                A0 = GL[(p % 2) * 2 + 0]
                A1 = GL[(p % 2) * 2 + 1]

                def _pv2(eng, p=p, A0=A0, A1=A1):
                    for A in (A0, A1):
                        eng.tensor_tensor(A["n1f"][:, 0:ch], A["gg"][:, 0:ch],
                                          A["f"][:, 0:ch], ALU.subtract)
                    eng.drain()
                    for A in (A0, A1):
                        eng.tensor_scalar(A["gg"][:, 0:ch], A["f"][:, 0:ch], 256.0,
                                          OFF15 + 0.5, ALU.mult, ALU.add)
                        eng.tensor_tensor(A["e"][:, 0:ch], A["n1f"][:, 0:ch],
                                          A["tmp"][:, 0:ch], ALU.mult)
                    eng.drain()
                    for A in (A0, A1):
                        eng.scalar_tensor_tensor(A["tmp"][:, 0:ch], A["e"][:, 0:ch], 4.0,
                                                 A["gg"][:, 0:ch], ALU.mult, ALU.add)
                    eng.drain()
                    for L2, A in ((0, A0), (1, A1)):
                        eng.tensor_copy(t_gfp[2 * p + L2][:, 0:ch], A["tmp"][:, 0:ch])
                tk_pv2[p] = P.step("vector", _pv2, waits=[tk_pgs[p]])

            for p in range(NP):
                for L2 in range(2):
                    mf = 2 * p + L2
                    base = w1_cnt[0]
                    for pq in range(2):
                        pk = (base + pq * 4) // 4
                        ii = mf * 8 + pq * 4
                        P.dma("sync", "w1_%d" % (pk % 4),
                              [(t_wb1[pk % 4][:].rearrange("p (b c) -> p b c", b=4),
                                i_w1[ii * 128:(ii + 4) * 128, :]
                                .rearrange("(b p) c -> p b c", b=4))],
                              waits=[tk_small] if pk < 4
                              else [Tok(P.csem("w1"), 4 * (pk - 4) + 4)])
                    w1_cnt[0] += 8

                    def _mm1(eng, bump, mf=mf, base=base):
                        for k in range(8):
                            i = base + k
                            pk = i // 4
                            eng.wait_ge(P.dma_sem("w1_%d" % (pk % 4)), 16 * (pk // 4 + 1))
                            r = eng.matmul(pA[mf % 2][0:128, 0:ch],
                                           t_wb1[pk % 4][:, 128 * (i % 4):128 * (i % 4) + 128],
                                           t_y[k][:, 0:ch], start=(k == 0), stop=(k == 7))
                            r.then_inc(P.csem("w1"), 1)
                    mm1_waits = [tk_h2]
                    if p >= 1:
                        mm1_waits.append(tk_pv1a[p - 1])
                    P.raw("tensor", _mm1, waits=mm1_waits)
                    tk_mm1p[mf] = Tok(P.csem("w1"), base + 8)

                A0 = GL[(p % 2) * 2 + 0]
                A1 = GL[(p % 2) * 2 + 1]

                def _pv1a(eng, p=p, A0=A0, A1=A1):
                    for L2, A in ((0, A0), (1, A1)):
                        mf = 2 * p + L2
                        eng.tensor_scalar(A["tmp"][:, 0:ch], pA[mf % 2][0:128, 0:ch], 256.0,
                                          t_bias1[:, mf:mf + 1], ALU.mult, ALU.add)
                w = [tk_mm1p[2 * p], tk_mm1p[2 * p + 1]]
                if p >= 2:
                    w.append(tk_pv2[p - 2])
                tk_pv1a[p] = P.step("vector", _pv1a, waits=w)

                def _pv1(eng, p=p, A0=A0, A1=A1):
                    for A in (A0, A1):
                        eng.tensor_copy(A["n1f"][:, 0:ch], A["tmp"][:, 0:ch])
                    eng.drain()
                    for A in (A0, A1):
                        eng.tensor_scalar(A["tmp"][:, 0:ch], A["n1f"][:, 0:ch], 1.0 / 64.0,
                                          None, ALU.mult)
                    eng.drain()
                    for A in (A0, A1):
                        eng.scalar_tensor_tensor(A["e"][:, 0:ch], A["tmp"][:, 0:ch], -64.0,
                                                 A["n1f"][:, 0:ch], ALU.mult, ALU.add)
                    eng.drain()
                    for A in (A0, A1):
                        eng.tensor_scalar(A["f"][:, 0:ch], A["e"][:, 0:ch], 0.0, 64.0,
                                          ALU.is_lt, ALU.mult)
                    eng.drain()
                    for A in (A0, A1):
                        eng.tensor_tensor(A["tmp"][:, 0:ch], A["e"][:, 0:ch], A["f"][:, 0:ch],
                                          ALU.add)
                    eng.drain()
                    for A in (A0, A1):
                        eng.scalar_tensor_tensor(A["e"][:, 0:ch], A["tmp"][:, 0:ch], -1.0,
                                                 A["n1f"][:, 0:ch], ALU.mult, ALU.add)
                tk_pv1[p] = P.step("vector", _pv1)

                def _pgs(eng, p=p, A0=A0, A1=A1):
                    for A in (A0, A1):
                        eng.activation(A["f"][:, 0:ch], A["e"][:, 0:ch], ACT.Gelu,
                                       scale=1.0 / 256.0, bias=t_cst[:, C_GB0:C_GB0 + 1])
                        eng.activation(A["gg"][:, 0:ch], A["e"][:, 0:ch], ACT.Gelu,
                                       scale=1.0 / 256.0, bias=t_cst[:, C_GB1:C_GB1 + 1])
                tk_pgs[p] = P.step("scalar", _pgs, waits=[tk_pv1[p]])

                if p >= 1:
                    emit_pv2(p - 1)
            emit_pv2(NP - 1)
            prev_ffB = tk_pv2[NP - 1]
            if "ffg" in dbg:
                P.dma("gpsimd", "dbg",
                      [(dbg["ffg"][mf * 128:(mf + 1) * 128, j * ch:(j + 1) * ch],
                        t_gfp[mf][:, 0:ch]) for mf in range(FG)], waits=[prev_ffB])

            tk_out = {}
            for md in range(DG):
                base = w2_cnt[0]
                for pq in range(FG // 4):
                    pk = (base + pq * 4) // 4
                    ii = md * FG + pq * 4
                    P.dma("sync", "w2_%d" % (pk % 4),
                          [(t_wb2[pk % 4][:].rearrange("p (b c) -> p b c", b=4),
                            i_w2[ii * 128:(ii + 4) * 128, :]
                            .rearrange("(b p) c -> p b c", b=4))],
                          waits=[tk_small] if pk < 4 else [Tok(P.csem("w2"), 4 * (pk - 4) + 4)])
                w2_cnt[0] += FG

                def _mm2(eng, bump, md=md, j=j, base=base):
                    for k in range(FG):
                        i = base + k
                        pk = i // 4
                        eng.wait_ge(P.dma_sem("w2_%d" % (pk % 4)), 16 * (pk // 4 + 1))
                        sl = t_wb2[pk % 4][:, 128 * (i % 4):128 * (i % 4) + 128]
                        r = eng.matmul(pB[md % 2][0:128, 0:ch], sl,
                                       t_gfp[k][:, 0:ch], start=(k == 0), stop=(k == FG - 1))
                        if j == 0:
                            bank = pSS if md % 2 == 0 else pRp
                            r = eng.matmul(bank[0:128, 0:1], sl, t_onesc[:],
                                           start=(k == 0), stop=(k == FG - 1))
                        r.then_inc(P.csem("w2"), 1)
                mm2_waits = [prev_ffB]
                if md >= 2:
                    mm2_waits.append(tk_out[md - 2])
                if j == 0 and md >= 2:
                    mm2_waits.append(tk_b2[md - 2])
                if j == 0 and md < 2:
                    mm2_waits.append(tk_h2)
                P.raw("tensor", _mm2, waits=mm2_waits)
                tk_mm2 = Tok(P.csem("w2"), base + FG)

                if j == 0:
                    def _b2c(eng, md=md):
                        bank = pSS if md % 2 == 0 else pRp
                        eng.scalar_tensor_tensor(t_bias2[:, md:md + 1], bank[0:128, 0:1],
                                                 -OFF15, t_b2p[:, md:md + 1],
                                                 ALU.mult, ALU.add)
                    tk_b2[md] = P.step("vector", _b2c, waits=[tk_mm2])

                def _outf(eng, md=md, j=j):
                    eng.tensor_scalar(t_n1[:, 0:ch], pB[md % 2][0:128, 0:ch], 1.0,
                                      t_bias2[:, md:md + 1], ALU.mult, ALU.add)
                    eng.drain()
                    eng.scalar_tensor_tensor(t_e[:, 0:ch], t_n1[:, 0:ch],
                                             t_gb256[:, md:md + 1], t_x1[md][:, 0:ch],
                                             ALU.mult, ALU.add)
                    eng.drain()
                    eng.tensor_scalar(t_x[md][:, 0:ch], t_e[:, 0:ch],
                                      t_gbfix[:, md:md + 1], None, ALU.add)
                ow = [tk_mm2]
                ow.append(tk_b2[md] if j == 0 else tk_bias2_done)
                tk_out[md] = P.step("vector", _outf, waits=ow)
            if j == 0:
                tk_bias2_done = tk_b2[DG - 1]
            prev_mm2 = tk_out[DG - 1]
            tk_store[j] = P.dma("sync", "store",
                                [(o_y[g * 128:(g + 1) * 128, j * ch:(j + 1) * ch],
                                  t_x[g][:, 0:ch]) for g in range(DG)],
                                waits=[tk_out[DG - 1]])

        def _bar(eng):
            for s in ("store", "dbg"):
                if s in P.dsems:
                    eng.wait_ge(P.dma_sem(s), P.dvals[s])
        P.step("sync", _bar)

    return nc



def build_fused(T_core, ch, debug_fe=(), debug_be=()):
    nc = bass.Bass("TRN2", target_bir_lowering=False)
    es = contextlib.ExitStack()
    with es:
        F = {"nc": nc}
        F["i_xT"] = nc.dram_tensor("xT", [D, T_core + 3], F32, kind="ExternalInput").ap()
        F["sp_cs"] = nc.dram_tensor("sp_cs", [NS * 128, T_core], F32).ap()
        F["sp_gate"] = nc.dram_tensor("sp_gate", [8 * 128, T_core], F32).ap()
        F["sp_den"] = nc.dram_tensor("sp_den", [32, T_core], F32).ap()
        F["cc_in"] = nc.dram_tensor("cc_in", [NCORE * 128, 66], F32)
        F["cc_out"] = nc.dram_tensor("cc_out", [NCORE * 128, 66], F32)
        F["cc_out"] = F["cc_out"]
        F["t_mod"] = es.enter_context(nc.sbuf_tensor("s_mod", [128, 48], F32))
        F["t_r32"] = es.enter_context(nc.sbuf_tensor("s_r32", [32, 8 * 128], F32))
        F["t_ones1"] = es.enter_context(nc.sbuf_tensor("s_ones1", [1, 128], F32))
        F["t_onesc"] = es.enter_context(nc.sbuf_tensor("s_onesc", [128, 1], F32))
        F["t_cst"] = es.enter_context(nc.sbuf_tensor("s_cst", [128, NCST], F32))
        F["block"] = es.enter_context(nc.Block())
        F["P"] = Prog(nc, F["block"], es)
        P = F["P"]

        build_fe(T_core, ch, debug=debug_fe, F=F)

        t_junk = es.enter_context(nc.sbuf_tensor("t_junk", [128, 1024], F32))

        def _cc(eng):
            eng.wait_ge(P.dma_sem("ccin"), P.dvals["ccin"])
            for i in range(NCORE):
                eng.dma_start(out=t_junk[:, 66 * i:66 * (i + 1)],
                              in_=F["cc_in"].ap()[i * 128:(i + 1) * 128, 0:66]
                              ).then_inc(P.csem("ccrb"), 16)
            eng.wait_ge(P.csem("ccrb"), 16 * NCORE)
            for i in range(NCORE):
                eng.dma_start(out=t_junk[:, 66 * i:66 * (i + 1)],
                              in_=F["cc_in"].ap()[i * 128:(i + 1) * 128, 0:66]
                              ).then_inc(P.csem("ccrb"), 16)
            eng.wait_ge(P.csem("ccrb"), 32 * NCORE)
            if CC_DELAY_PRE:
                for _ in range(CC_DELAY_PRE):
                    eng.memset(t_junk[:], 0.0)
                    eng.drain()
            r = eng.collective_compute(
                "AllReduce", ALU.add, replica_groups=[list(range(NCORE))],
                ins=[F["cc_in"].ap().opt()], outs=[F["cc_out"].ap().opt()])
            if CC_DELAY_POST:
                r.then_inc(P.csem("ccflush"), 1)
                eng.wait_ge(P.csem("ccflush"), 1)
                for _ in range(CC_DELAY_POST):
                    eng.memset(t_junk[:], 0.0)
                    eng.drain()
                return None
            return r
        F["cc_done"] = P.step("gpsimd", _cc, waits=F["fe_bar"], attach=True)
        if DEBUG_CC:
            dci = nc.dram_tensor("dbg_ccin", [NCORE * 128, 66], F32, kind="ExternalOutput").ap()
            dco = nc.dram_tensor("dbg_ccout", [NCORE * 128, 66], F32, kind="ExternalOutput").ap()
            P.dma("gpsimd", "dbg", [(dci, F["cc_in"].ap()), (dco, F["cc_out"].ap())],
                  waits=[F["cc_done"]])

        build_be(T_core, ch, debug=debug_be, F=F)
    return nc



def _tile_rows(w, nblk_r, nblk_c, bs_r=128, bs_c=128):
    R, C = w.shape
    out = np.zeros((nblk_r * nblk_c * bs_r, bs_c), dtype=w.dtype)
    for i in range(nblk_r):
        for jj in range(nblk_c):
            blk = w[i * bs_r:(i + 1) * bs_r, jj * bs_c:(jj + 1) * bs_c]
            r0 = (i * nblk_c + jj) * bs_r
            out[r0:r0 + blk.shape[0], 0:blk.shape[1]] = blk
    return out


def _fcol(v, parts=128):
    n = v.shape[0]
    k = n // parts
    return np.ascontiguousarray(v.reshape(k, parts).T)


_FE_CACHE = {}
_BE_CACHE = {}


def kernel(L0_attn_x, L0_attn_W_in, L0_attn_w_conv, L0_attn_b_conv, L0_attn_theta,
           L0_attn_decay_slopes, L0_attn_anchor_slopes, L0_attn_score_scale,
           L0_attn_deriv_logits, L0_attn_norm_scale, L0_attn_W_re, L0_attn_W_im,
           L0_attn_W_out, L0_ffn_w1, L0_ffn_b1, L0_ffn_w2, L0_ffn_b2, L0_c, L0_wmod,
           _debug_fe=(), _debug_be=(), _t_core=T_CORE_DEFAULT, _ch=CH_DEFAULT,
           _fe_only=False, _fused=True):
    T_core = _t_core
    ch = _ch
    x = np.ascontiguousarray(np.asarray(L0_attn_x, np.float32))
    B, L, _ = x.shape
    nq = NCORE // B
    assert T_core * nq == L or T_core != T_CORE_DEFAULT

    f32 = lambda a: np.ascontiguousarray(np.asarray(a, np.float32))
    W_in = f32(L0_attn_W_in)
    w_conv = f32(L0_attn_w_conv)
    b_conv = f32(L0_attn_b_conv)
    theta = f32(L0_attn_theta)
    dsl = f32(L0_attn_decay_slopes)
    asl = f32(L0_attn_anchor_slopes)
    score = f32(L0_attn_score_scale)
    dlog = f32(L0_attn_deriv_logits)
    nsc = f32(L0_attn_norm_scale)
    W_re = f32(L0_attn_W_re)
    W_im = f32(L0_attn_W_im)
    W_out = f32(L0_attn_W_out)
    w1 = f32(L0_ffn_w1)
    b1 = f32(L0_ffn_b1)
    w2 = f32(L0_ffn_w2)
    b2 = f32(L0_ffn_b2)
    cmat = f32(L0_c)
    wmod = f32(L0_wmod)

    win_t = _tile_rows(W_in, ZG, 8, 128, 128)
    win_t = np.zeros((ZG * 8 * 128, 128), np.float32)
    for mz in range(ZG):
        for k in range(8):
            blk = W_in[k * 128:(k + 1) * 128, mz * 128:mz * 128 + _zsz(mz)]
            r0 = (mz * 8 + k) * 128
            win_t[r0:r0 + 128, 0:blk.shape[1]] = blk
    wmod_t = np.zeros((12 * 8 * 128, 512), np.float32)
    for cc in range(12):
        for k in range(8):
            wmod_t[(cc * 8 + k) * 128:(cc * 8 + k + 1) * 128, :] = \
                wmod[k * 128:(k + 1) * 128, cc * 512:(cc + 1) * 512]
    wcv = np.zeros((128, 4 * ZG), np.float32)
    bcv = np.zeros((128, ZG), np.float32)
    for mz in range(ZG):
        zs = _zsz(mz)
        for jj in range(4):
            wcv[0:zs, jj * ZG + mz] = w_conv[jj, 0, mz * 128:mz * 128 + zs]
        bcv[0:zs, mz] = b_conv[mz * 128:mz * 128 + zs]
    th_flat = theta.reshape(D, MTH)
    thetas = np.zeros((128, 32), np.float32)
    for g in range(8):
        for m in range(MTH):
            thetas[:, g * 4 + m] = th_flat[g * 128:(g + 1) * 128, m]
    slvec = np.zeros((32, 2), np.float32)
    slvec[:, 0] = np.concatenate([dsl, asl])
    slvec[:28, 1] = 1.0
    slvec[28:, 1] = -1.0
    scorev = score.reshape(32, 1)
    dlog2 = dlog.reshape(1, 3)
    hm1 = np.ones((128, 3), np.float32)
    hm0 = np.zeros((128, 3), np.float32)
    r32 = np.zeros((32, 8 * 128), np.float32)
    for g in range(8):
        for c_ in range(128):
            r32[4 * g + c_ // 32, g * 128 + c_] = 1.0
    ones1 = np.ones((1, 128), np.float32)
    onesc = np.ones((128, 1), np.float32)
    cstrow = np.zeros(NCST, np.float32)
    cstrow[C_EPSLN] = np.float32(1e-5)
    cstrow[C_EPSRMS] = np.float32(1e-5)
    cstrow[C_ONE] = 1.0
    cstrow[C_GB0] = -128.0
    cstrow[C_GB1] = -127.75
    cst = np.tile(cstrow[None, :], (128, 1)).astype(np.float32)

    w1_t = np.zeros((FG * 8 * 128, 128), np.float32)
    for mf in range(FG):
        for k in range(8):
            w1_t[(mf * 8 + k) * 128:(mf * 8 + k + 1) * 128, :] = \
                w1[k * 128:(k + 1) * 128, mf * 128:(mf + 1) * 128]
    w2_t = np.zeros((8 * FG * 128, 128), np.float32)
    for md in range(8):
        for k in range(FG):
            w2_t[(md * FG + k) * 128:(md * FG + k + 1) * 128, :] = \
                w2[k * 128:(k + 1) * 128, md * 128:(md + 1) * 128]
    wo_t = np.zeros((8 * 8 * 128, 128), np.float32)
    for md in range(8):
        for k in range(8):
            wo_t[(md * 8 + k) * 128:(md * 8 + k + 1) * 128, :] = \
                W_out[k * 128:(k + 1) * 128, md * 128:(md + 1) * 128]
    wrebd = np.zeros((8 * 128, 128), np.float32)
    nsccols = np.zeros((128, 8), np.float32)
    for m in range(MTH):
        for side in range(2):
            i = 2 * m + side
            Wsrc = W_re if side == 0 else W_im
            Wm = Wsrc[m::MTH, :]
            blk = np.zeros((128, 128), np.float32)
            for kk in range(4):
                blk[kk * 32:(kk + 1) * 32, kk * 32:(kk + 1) * 32] = Wm
            wrebd[i * 128:(i + 1) * 128, :] = blk
            for p in range(128):
                nsccols[p, i] = nsc[side * 128 + (p % 32) * MTH + m]
    obd32 = np.zeros((128, 8 * 32), np.float32)
    for g in range(8):
        for p_ in range(128):
            obd32[p_, g * 32 + 4 * g + p_ // 32] = 1.0
    b1c = _fcol(b1)
    b2c = _fcol(b2)

    if _fused:
        key = ("fused", T_core, ch, tuple(_debug_fe), tuple(_debug_be))
        if key not in _FE_CACHE:
            _FE_CACHE[key] = build_fused(T_core, ch, _debug_fe, _debug_be)
        nc_f = _FE_CACHE[key]
    else:
        key = (T_core, ch, tuple(_debug_fe))
        if key not in _FE_CACHE:
            _FE_CACHE[key] = build_fe(T_core, ch, _debug_fe)
        nc_fe = _FE_CACHE[key]

    in_maps = []
    for core in range(NCORE):
        b, q = core // nq, core % nq
        t0 = q * T_core
        if t0 == 0:
            xs = np.concatenate([np.zeros((3, D), np.float32), x[b, 0:T_core + 0]], axis=0)
            hm = hm0
        else:
            xs = x[b, t0 - 3:t0 + T_core]
            hm = hm1
        xT = np.ascontiguousarray(xs.T)
        tb = np.zeros((32, 2), np.float32)
        tb[:28, 0] = -float(L - 1 - t0)
        tb[28:, 0] = float(t0)
        in_maps.append({
            "xT": xT, "win": win_t, "wmod": wmod_t, "wcv": wcv, "bcv": bcv,
            "thetas": thetas, "slvec": slvec, "scorev": scorev, "dlog": dlog2,
            "crow": np.ascontiguousarray(cmat[b].reshape(8, 128).T),
            "hmask": hm, "r32": r32, "ones_k1": ones1, "ones_col": onesc,
            "tbase": tb, "cst": cst,
        })
        if _fused:
            wsl = np.zeros((128, NCORE), np.float32)
            wsl[:, core] = 1.0
            smask = np.zeros((128, NCORE), np.float32)
            for i in range(NCORE):
                ib_, iq_ = i // nq, i % nq
                if ib_ == b and iq_ < q:
                    smask[:, i] = 1.0
            in_maps[-1].update({
                "smask": smask, "wsl": wsl,
                "w1t": w1_t, "w2t": w2_t, "wot": wo_t, "wrebd": wrebd,
                "nsccols": nsccols, "obd32": obd32, "b1c": b1c, "b2c": b2c,
            })

    if _fused:
        res = run_bass_kernel_spmd(nc_f, in_maps, core_ids=list(range(NCORE)))
        be = res.results
        if _debug_fe or _debug_be:
            return be
        out = np.empty((B, L, D), np.float32)
        for core in range(NCORE):
            b, q = core // nq, core % nq
            out[b, q * T_core:(q + 1) * T_core, :] = be[core]["yT"].T
        return out

    res_fe = run_bass_kernel_spmd(nc_fe, in_maps, core_ids=list(range(NCORE)))
    fe = res_fe.results
    if _fe_only:
        return fe

    s_all = np.concatenate([fe[i]["s_out"] for i in range(NCORE)], axis=0)

    key = (T_core, ch, tuple(_debug_be))
    if key not in _BE_CACHE:
        _BE_CACHE[key] = build_be(T_core, ch, _debug_be)
    nc_be = _BE_CACHE[key]

    in_maps2 = []
    for core in range(NCORE):
        b, q = core // nq, core % nq
        t0 = q * T_core
        smask = np.zeros((128, NCORE), np.float32)
        for i in range(NCORE):
            ib, iq = i // nq, i % nq
            if ib == b and iq < q:
                smask[:, i] = 1.0
        in_maps2.append({
            "xT": in_maps[core]["xT"],
            "sp_cs": fe[core]["sp_cs"], "sp_gate": fe[core]["sp_gate"],
            "sp_den": fe[core]["sp_den"], "s_all": s_all, "smask": smask,
            "mod_row": fe[core]["mod_out"],
            "w1t": w1_t, "w2t": w2_t, "wot": wo_t, "wrebd": wrebd,
            "nsccols": nsccols, "obd32": obd32, "b1c": b1c, "b2c": b2c,
            "r32": r32, "ones_k1": ones1, "ones_col": onesc, "cst": cst,
        })

    res_be = run_bass_kernel_spmd(nc_be, in_maps2, core_ids=list(range(NCORE)))
    be = res_be.results
    if _debug_be:
        return fe, be

    out = np.empty((B, L, D), np.float32)
    for core in range(NCORE):
        b, q = core // nq, core % nq
        out[b, q * T_core:(q + 1) * T_core, :] = be[core]["yT"].T
    return out
